# version 01
# baseline (speedup 1.0000x reference)
import contextlib
import numpy as np
import ml_dtypes
import concourse.bass as bass
import concourse.mybir as mybir
from concourse.bass_utils import run_bass_kernel_spmd

F32 = mybir.dt.float32
BF16 = mybir.dt.bfloat16
FP8 = mybir.dt.float8e4
FP8E5 = mybir.dt.float8e5
AF = mybir.ActivationFunctionType
ALU = mybir.AluOpType
AX = mybir.AxisListType

D = 4096
S = 2048
H = 32
KCH = 32
R = 16
E = 8
SCALE_Q = float(192) ** -0.5
EPS = 1e-5
NEG = -1e30
NEG8 = -57344.0

ENGINES = ("te", "ve", "se", "ge", "sy")


class Sem:
    def __init__(self, handle):
        self.h = handle
        self.n = 0


class Prog:
    def __init__(self):
        self.q = {e: [] for e in ENGINES}

    def op(self, eng, method, *args, waits=(), inc=None, **kwargs):
        waits = tuple((s, v) for (s, v) in waits if v > 0)
        self.q[eng].append((method, args, kwargs, waits, inc))
        if inc is not None:
            inc[0].n += inc[1]

    def wait(self, eng, sem, val):
        if val > 0:
            self.q[eng].append(("wait_ge", (sem, val), {}, (), None))

    def run(self, engine_obj, eng):
        for method, args, kwargs, waits, inc in self.q[eng]:
            for sem, val in waits:
                engine_obj.wait_ge(sem.h, val)
            if method == "wait_ge":
                engine_obj.wait_ge(args[0].h, args[1])
                continue
            ins = getattr(engine_obj, method)(*args, **kwargs)
            if inc is not None:
                ins.then_inc(inc[0].h, inc[1])


def own_tiles(parity):
    return [4 * g + 2 * parity + j for g in range(4) for j in range(2)]


def build_masks(parity):
    ts = own_tiles(parity)
    pts = own_tiles(1 - parity)
    m = np.zeros((8, 2, 128, 256), np.float32)
    for l in range(8):
        g = l // 2
        tq = ts[l]
        for w in range(2):
            src = ts if w == 0 else pts
            for ci, mloc in enumerate((2 * g, 2 * g + 1)):
                tk = src[mloc]
                qpos = 128 * tq + np.arange(128)[:, None]
                kpos = 128 * tk + np.arange(128)[None, :]
                m[l, w, :, ci * 128:(ci + 1) * 128] = np.where(
                    kpos <= qpos, 0.0, NEG)
    return m.astype(ml_dtypes.bfloat16)


def bank_pieces(a, b):
    out = []
    while a < b:
        nxt = min(b, (a // 512 + 1) * 512)
        out.append((a, nxt))
        a = nxt
    return out


def build_program(debug=False, stage=99):
    nc = bass.Bass("TRN2", target_bir_lowering=False)
    p = Prog()

    x_in = nc.dram_tensor("x_in", [S, D], F32, kind="ExternalInput").ap()
    wq_d = nc.dram_tensor("wq", [D, H * 192], F32, kind="ExternalInput").ap()
    wkv_d = nc.dram_tensor("wkv", [D, 8256], F32, kind="ExternalInput").ap()
    wo_d = nc.dram_tensor("wo", [D, D], F32, kind="ExternalInput").ap()
    gate_d = nc.dram_tensor("gate_w", [E, D], F32, kind="ExternalInput").ap()
    base_d = nc.dram_tensor("base_w", [D, D], F32, kind="ExternalInput").ap()
    bias_d = nc.dram_tensor("base_b", [1, D], F32, kind="ExternalInput").ap()
    lA_d = nc.dram_tensor("lora_A", [E * R, D], F32, kind="ExternalInput").ap()
    lB_d = nc.dram_tensor("lora_B", [E, D, R], F32, kind="ExternalInput").ap()
    ident_d = nc.dram_tensor("ident", [128, 128], BF16, kind="ExternalInput").ap()
    ones_d = nc.dram_tensor("ones1", [1, 128], BF16, kind="ExternalInput").ap()
    masks_d = nc.dram_tensor("masks", [16 * 128, 256], BF16,
                             kind="ExternalInput").ap()

    y_d = nc.dram_tensor("y", [1024, D], F32, kind="ExternalOutput").ap()
    dbg = {}
    if debug:
        dbg["dbg_x1"] = nc.dram_tensor("dbg_x1", [S, D], F32, kind="ExternalOutput").ap()
        dbg["dbg_x2"] = nc.dram_tensor("dbg_x2", [1024, D], F32, kind="ExternalOutput").ap()
        dbg["dbg_kT"] = nc.dram_tensor("dbg_kT", [H * 128, S], BF16, kind="ExternalOutput").ap()
        dbg["dbg_v"] = nc.dram_tensor("dbg_v", [S, H * 128], BF16, kind="ExternalOutput").ap()
        dbg["dbg_qT"] = nc.dram_tensor("dbg_qT", [H * 128, 1024], BF16, kind="ExternalOutput").ap()
        dbg["dbg_bwT"] = nc.dram_tensor("dbg_bwT", [D, D], BF16, kind="ExternalOutput").ap()
        dbg["dbg_xn"] = nc.dram_tensor("dbg_xn", [S, D], BF16, kind="ExternalOutput").ap()
        dbg["dbg_at"] = nc.dram_tensor("dbg_at", [128, 32 * 1024], BF16, kind="ExternalOutput").ap()
        dbg["dbg_pt"] = nc.dram_tensor("dbg_pt", [128, 16 * 1024], BF16, kind="ExternalOutput").ap()
        dbg["dbg_psb"] = nc.dram_tensor("dbg_psb", [128, 2048], BF16, kind="ExternalOutput").ap()
        dbg["dbg_sc"] = nc.dram_tensor("dbg_sc", [128, 2048], F32, kind="ExternalOutput").ap()
        dbg["dbg_ss"] = nc.dram_tensor("dbg_ss", [128, 28], F32, kind="ExternalOutput").ap()
        dbg["dbg_wf"] = nc.dram_tensor("dbg_wf", [128, 64], F32, kind="ExternalOutput").ap()
        dbg["dbg_lwT"] = nc.dram_tensor("dbg_lwT", [128, 2048], BF16, kind="ExternalOutput").ap()
        dbg["dbg_gs"] = nc.dram_tensor("dbg_gs", [128, 28], F32, kind="ExternalOutput").ap()
        dbg["dbg_gl"] = nc.dram_tensor("dbg_gl", [128, 8], F32, kind="ExternalOutput").ap()

    x1_d = nc.dram_tensor("x1", [S, D], F32).ap()
    x2_d = nc.dram_tensor("x2", [1024, D], F32).ap()
    bwT_d = nc.dram_tensor("bwT", [D, D], BF16).ap()
    kT_d = nc.dram_tensor("kT", [H * 128, S], BF16).ap()
    v_d = nc.dram_tensor("v", [S, H * 128], BF16).ap()
    qT_d = nc.dram_tensor("qT", [H * 128, 1024], BF16).ap()

    ctx = contextlib.ExitStack()
    sbt = lambda name, shape, dt: ctx.enter_context(nc.sbuf_tensor(name, shape, dt))
    pst = lambda name, shape, dt: ctx.enter_context(nc.psum_tensor(name, shape, dt))
    mksem = lambda name: Sem(ctx.enter_context(nc.semaphore(name)))

    with ctx:
        s_lw = mksem("s_lw")
        s_lx = mksem("s_lx")
        s_sx = mksem("s_sx")
        s_pe = mksem("s_pe")
        s_ve = mksem("s_ve")
        s_ac = mksem("s_ac")
        s_gp = mksem("s_gp")
        ALLSEMS = (s_lw, s_lx, s_sx, s_pe, s_ve, s_ac, s_gp)

        ar1 = sbt("ar1", [128, 32 * 1024], BF16)
        ar3 = sbt("ar3", [128, 32 * 1024], BF16)
        xst = sbt("xst", [128, 4096], F32)
        wst = sbt("wst", [128, 1024], F32)
        xnst = sbt("xnst", [128, 4096], BF16)
        est = sbt("est", [128, 2 * 512], F32)
        bst = sbt("bst", [128, 4 * 512], BF16)
        junk = sbt("junk", [128, 4096], FP8)
        stats = sbt("stats", [128, 28], F32)
        bflat = sbt("bflat", [128, D], BF16)
        aflat = sbt("aflat", [128, KCH * 128], BF16)
        gwT = sbt("gwT", [128, KCH * 8], BF16)
        lwT = sbt("lwT", [128, S], BF16)
        wfull = sbt("wfull", [128, 8 * 8], F32)
        brow = sbt("brow", [1, D], BF16)
        ident = sbt("ident_s", [128, 128], BF16)
        ones1 = sbt("ones1_s", [1, 128], BF16)
        masksb = sbt("masksb_s", [128, 16 * 256], BF16)

        pS = pst("pS", [128, 2048], F32)
        pX = pst("pX", [128, 1024], BF16)
        pA = pst("pA", [128, 512], F32)
        pB = pst("pB", [128, 512], F32)
        pC = pst("pC", [128, 512], F32)

        def psq(i):
            return pS[:, 512 * i:512 * (i + 1)]

        def fence():
            for eng in ENGINES:
                for s in ALLSEMS:
                    p.wait(eng, s, s.n)

        p.op("sy", "dma_start", out=ident[:, :], in_=ident_d, inc=(s_lw, 16))
        p.op("sy", "dma_start", out=ones1[:, :], in_=ones_d, inc=(s_lw, 16))
        p.op("sy", "dma_start",
             out=masksb[:, :].rearrange("r (l k) -> r l k", l=16),
             in_=masks_d.rearrange("(l r) k -> r l k", r=128),
             inc=(s_lw, 16))
        p.op("sy", "dma_start", out=xst[0:1, 0:D], in_=bias_d, inc=(s_lw, 16))
        p.wait("ge", s_lw, s_lw.n)
        p.op("ge", "tensor_copy", brow[:, :], xst[0:1, 0:D], inc=(s_gp, 1))
        p.op("ge", "memset", stats[:, 24:25], EPS, inc=(s_gp, 1))

        for piece in range(4 if (stage >= -4 and stage != -20 and stage != -22) else 0):
            p.op("sy", "dma_start", out=wst[0:8, 0:1024],
                 in_=gate_d[:, 1024 * piece:1024 * (piece + 1)],
                 waits=[(s_gp, s_gp.n)], inc=(s_lw, 16))
            p.wait("ge", s_lw, s_lw.n)
            p.wait("ge", s_pe, s_pe.n)
            p.op("ge", "tensor_copy", bst[0:8, 0:1024], wst[0:8, 0:1024],
                 inc=(s_gp, 1))
            p.wait("te", s_gp, s_gp.n)
            p.wait("te", s_ac, s_ac.n)
            for i in range(8):
                p.op("te", "transpose", pX[:, i * 8:(i + 1) * 8],
                     bst[0:8, i * 128:(i + 1) * 128], ident[0:8, 0:8],
                     inc=(s_pe, 1) if i == 7 else None)
            p.wait("se", s_pe, s_pe.n)
            p.op("se", "activation", gwT[:, piece * 64:(piece + 1) * 64],
                 pX[:, 0:64], AF.Copy, inc=(s_ac, 1))

        for piece in range(4 if (stage >= -3 and stage not in (-20, -21)) else 0):
            p.op("sy", "dma_start", out=wst[:, 0:1024],
                 in_=lA_d[:, 1024 * piece:1024 * (piece + 1)],
                 waits=[(s_gp, s_gp.n)], inc=(s_lw, 16))
            p.wait("ge", s_lw, s_lw.n)
            p.op("ge", "tensor_copy", ar1[:, piece * 1024:(piece + 1) * 1024],
                 wst[:, 0:1024], inc=(s_gp, 1))
        for fill in range(4 if (stage >= -3 and stage not in (-20, -21)) else 0):
            p.wait("te", s_gp, s_gp.n)
            p.wait("te", s_ac, s_ac.n)
            for i in range(8):
                k = fill * 8 + i
                p.op("te", "transpose", pX[:, i * 128:(i + 1) * 128],
                     ar1[:, k * 128:(k + 1) * 128], ident[:, :],
                     inc=(s_pe, 1) if i == 7 else None)
            p.wait("se", s_pe, s_pe.n)
            p.op("se", "activation", aflat[:, fill * 1024:(fill + 1) * 1024],
                 pX[:, :], AF.Copy, inc=(s_ac, 1))

        for e in range(E if (stage >= -2 or stage in (-20, -21, -22)) else 0):
            p.op("sy", "dma_start",
                 out=wst[:, 0:512].rearrange("p (c r) -> p c r", c=KCH),
                 in_=lB_d[e].rearrange("(c p) r -> p c r", p=128),
                 waits=[(s_gp, s_gp.n)], inc=(s_lw, 16))
            p.wait("ge", s_lw, s_lw.n)
            p.wait("ge", s_pe, s_pe.n)
            p.op("ge", "tensor_copy", bst[:, 0:512], wst[:, 0:512],
                 inc=(s_gp, 1))
            for fill in range(4):
                p.wait("te", s_gp, s_gp.n)
                p.wait("te", s_ve, s_ve.n)
                p.wait("te", s_ac, s_ac.n)
                for i in range(8):
                    c = fill * 8 + i
                    p.op("te", "transpose", pX[0:16, i * 128:(i + 1) * 128],
                         bst[:, c * 16:(c + 1) * 16], ident[:, :],
                         inc=(s_pe, 1) if i == 7 else None)
                p.wait("ve", s_pe, s_pe.n)
                p.wait("ve", s_lx, s_lx.n)
                p.op("ve", "tensor_copy", bst[0:16, 1024:2048],
                     pX[0:16, 0:1024], inc=(s_ve, 1))
                p.wait("ge", s_ve, s_ve.n)
                p.op("ge", "dma_start",
                     out=bflat[16 * e:16 * (e + 1),
                               fill * 1024:(fill + 1) * 1024],
                     in_=bst[0:16, 1024:2048], inc=(s_lx, 16))
        fence()

        wst_free = [s_gp.n, s_gp.n]
        junk_free = [s_sx.n, s_sx.n]
        bst_half_pe = [0, 0]
        it = 0
        for Bk in range(8 if stage >= -1 else 0):
            for q in range(4):
                for dp in range(4):
                    half = it % 2
                    wslc = wst[:, 0:1024]
                    bslc = bst[:, half * 1024:(half + 1) * 1024]
                    jslc = xnst[:, half * 1024:(half + 1) * 1024]
                    p.op("sy", "dma_start", out=wslc,
                         in_=base_d[512 * Bk + 128 * q:512 * Bk + 128 * (q + 1),
                                    1024 * dp:1024 * (dp + 1)],
                         waits=[(s_gp, wst_free[0])], inc=(s_lw, 16))
                    p.wait("ge", s_lw, s_lw.n)
                    p.wait("ge", s_pe, bst_half_pe[half])
                    p.op("ge", "tensor_copy", bslc, wslc, inc=(s_gp, 1))
                    wst_free[0] = s_gp.n
                    p.wait("te", s_gp, s_gp.n)
                    p.wait("te", s_ac, s_ac.n)
                    for i in range(8):
                        p.op("te", "transpose", pX[:, i * 128:(i + 1) * 128],
                             bslc[:, i * 128:(i + 1) * 128], ident[:, :],
                             inc=(s_pe, 1) if i == 7 else None)
                    bst_half_pe[half] = s_pe.n
                    p.wait("se", s_pe, s_pe.n)
                    p.wait("se", s_sx, junk_free[half])
                    p.op("se", "activation", jslc, pX[:, :], AF.Copy,
                         inc=(s_ac, 1))
                    p.wait("se", s_ac, s_ac.n)
                    for i in range(8):
                        p.op("se", "dma_start",
                             out=bwT_d[1024 * dp + 128 * i:
                                       1024 * dp + 128 * (i + 1),
                                       512 * Bk + 128 * q:
                                       512 * Bk + 128 * (q + 1)],
                             in_=jslc[:, i * 128:(i + 1) * 128],
                             inc=(s_sx, 16))
                    junk_free[half] = s_sx.n
                    it += 1
        fence()

        est_free = [s_sx.n] * 2
        bst_free = [s_sx.n] * 4
        xslot_free = [s_ve.n] * 8
        psum_free = {0: s_ve.n, 1: s_ve.n}

        def ln_tiles(src, src_off, ntiles, dump=None):
            ar1_3d = ar1[:, :].rearrange("p (k c) -> p k c", k=KCH)
            prev_tr_pe = s_pe.n
            p.op("ve", "memset", stats[:, 8:9], 0.0)
            for t in range(ntiles):
                r0 = src_off + 128 * t
                p.wait("ge", s_ve, s_ve.n)
                p.op("ge", "dma_start", out=xst[:, 0:2048],
                     in_=src[r0:r0 + 128, 0:2048], inc=(s_lx, 16))
                p.op("ge", "dma_start", out=xst[:, 2048:4096],
                     in_=src[r0:r0 + 128, 2048:4096], inc=(s_lx, 16))
                p.wait("ve", s_lx, s_lx.n)
                p.op("ve", "reduce_sum", stats[:, 0:1], xst[:, :], axis=AX.X,
                     inc=(s_ve, 1))
                p.wait("ve", s_ve, s_ve.n)
                p.op("ve", "engine_nop", inc=(s_ve, 1))
                p.wait("se", s_ve, s_ve.n)
                p.op("se", "activation", stats[:, 1:2], stats[:, 0:1],
                     AF.Copy, scale=-1.0 / D, inc=(s_ac, 1))
                p.wait("ve", s_ac, s_ac.n)
                p.op("ve", "tensor_scalar", stats[:, 9:10], stats[:, 8:9],
                     0.0, 0.0, ALU.add, ALU.add, accum_out=stats[:, 10:11])
                p.op("ve", "scalar_tensor_tensor", junk[:, :], xst[:, :],
                     stats[:, 1:2], xst[:, :], ALU.add, ALU.mult,
                     accum_out=stats[:, 2:3], inc=(s_ve, 1))
                p.wait("ve", s_ve, s_ve.n)
                p.op("ve", "engine_nop", inc=(s_ve, 1))
                p.wait("se", s_ve, s_ve.n)
                p.op("se", "activation", stats[:, 3:4], stats[:, 2:3], AF.Ln,
                     scale=1.0 / D, bias=stats[:, 24:25])
                p.op("se", "activation", stats[:, 4:5], stats[:, 3:4], AF.Exp,
                     scale=-0.5, inc=(s_ac, 1))
                p.wait("ve", s_ac, s_ac.n)
                p.wait("ve", s_pe, prev_tr_pe)
                if dump is not None:
                    p.wait("ve", s_sx, s_sx.n)
                p.op("ve", "tensor_scalar", xnst[:, :], xst[:, :],
                     stats[:, 1:2], stats[:, 4:5], ALU.add, ALU.mult,
                     inc=(s_ve, 1))
                if dump is not None:
                    p.wait("se", s_ve, s_ve.n)
                    p.op("se", "dma_start",
                         out=dump[src_off + 128 * t:src_off + 128 * (t + 1), :],
                         in_=xnst[:, :], inc=(s_sx, 16))
                p.wait("te", s_ve, s_ve.n)
                for fill in range(4):
                    p.wait("te", s_ac, s_ac.n)
                    for i in range(8):
                        k = fill * 8 + i
                        p.op("te", "transpose", pX[:, i * 128:(i + 1) * 128],
                             xnst[:, k * 128:(k + 1) * 128], ident[:, :],
                             inc=(s_pe, 1) if i == 7 else None)
                    p.wait("se", s_pe, s_pe.n)
                    p.op("se", "activation",
                         ar1_3d[:, fill * 8:(fill + 1) * 8,
                                t * 128:(t + 1) * 128],
                         pX[:, :], AF.Copy, inc=(s_ac, 1))
                prev_tr_pe = s_pe.n

        def xn_chunk(k, c0, w):
            return ar1[:, k * 1024 + c0:k * 1024 + c0 + w]

        def ffn_half(src, dst, res_base, lw_off, bw_ready_sx, dbg_gate=None):
            for t in range(8):
                p.wait("te", s_ve, s_ve.n)
                for k in range(KCH):
                    p.op("te", "matmul", pC[:, 0:8],
                         xn_chunk(k, t * 128, 128), gwT[:, k * 8:(k + 1) * 8],
                         start=(k == 0), stop=(k == KCH - 1),
                         inc=(s_pe, 1) if k == KCH - 1 else None)
                p.wait("ve", s_pe, s_pe.n)
                p.op("ve", "reduce_max", stats[:, 5:6], pC[:, 0:8],
                     axis=AX.X, negate=True, inc=(s_ve, 1))
                p.wait("ve", s_ve, s_ve.n)
                p.op("ve", "tensor_scalar", stats[:, 8:16], pC[:, 0:8],
                     stats[:, 5:6], 0.0, ALU.add, ALU.is_equal, inc=(s_ve, 1))
                p.wait("ve", s_ve, s_ve.n)
                p.op("ve", "scalar_tensor_tensor", stats[:, 16:24],
                     stats[:, 8:16], NEG, pC[:, 0:8], ALU.mult, ALU.add,
                     inc=(s_ve, 1))
                p.wait("ve", s_ve, s_ve.n)
                p.op("ve", "reduce_max", stats[:, 6:7], stats[:, 16:24],
                     axis=AX.X, negate=True, inc=(s_ve, 1))
                p.wait("ve", s_ve, s_ve.n)
                p.op("ve", "tensor_scalar", stats[:, 8:16], pC[:, 0:8],
                     stats[:, 6:7], 0.0, ALU.add, ALU.is_ge, inc=(s_ve, 1))
                p.wait("ve", s_ve, s_ve.n)
                p.op("ve", "tensor_tensor", stats[:, 0:1], stats[:, 5:6],
                     stats[:, 6:7], ALU.subtract, inc=(s_ve, 1))
                p.wait("ve", s_ve, s_ve.n)
                p.op("ve", "engine_nop", inc=(s_ve, 1))
                p.wait("se", s_ve, s_ve.n)
                p.op("se", "activation", stats[:, 16:24], pC[:, 0:8], AF.Exp,
                     bias=stats[:, 5:6])
                p.op("se", "activation", stats[:, 1:2], stats[:, 0:1],
                     AF.Exp, inc=(s_ac, 1))
                p.wait("ve", s_ac, s_ac.n)
                p.op("ve", "tensor_scalar", stats[:, 2:3], stats[:, 1:2],
                     1.0, None, ALU.add, inc=(s_ve, 1))
                p.wait("ve", s_ve, s_ve.n)
                p.op("ve", "reciprocal", stats[:, 3:4], stats[:, 2:3],
                     inc=(s_ve, 1))
                p.wait("ve", s_ve, s_ve.n)
                p.op("ve", "scalar_tensor_tensor", wfull[:, t * 8:(t + 1) * 8],
                     stats[:, 16:24], stats[:, 3:4], stats[:, 8:16],
                     ALU.mult, ALU.mult, inc=(s_ve, 1))
                if dbg_gate is not None and t == 7:
                    p.op("ve", "tensor_copy", est[:, 0:8], pC[:, 0:8],
                         inc=(s_ve, 1))
                    p.wait("ge", s_ve, s_ve.n)
                    p.op("ge", "dma_start", out=dbg_gate[0], in_=est[:, 0:8],
                         inc=(s_lx, 16))
                    p.op("ge", "dma_start", out=dbg_gate[1], in_=stats[:, :],
                         inc=(s_lx, 16))
                    p.wait("te", s_lx, s_lx.n)
            for t in range(8):
                pi = t % 2
                ps = pA if pi == 0 else pB
                p.wait("te", s_ve, psum_free[pi])
                for k in range(KCH):
                    p.op("te", "matmul", ps[:, 0:128],
                         xn_chunk(k, t * 128, 128),
                         aflat[:, k * 128:(k + 1) * 128],
                         start=(k == 0), stop=(k == KCH - 1),
                         inc=(s_pe, 1) if k == KCH - 1 else None)
                wexp = wfull[:, t * 8:(t + 1) * 8].unsqueeze(2) \
                    .broadcast_to([128, 8, 16])
                p.wait("ve", s_pe, s_pe.n)
                p.op("ve", "scalar_tensor_tensor",
                     xnst[:, t * 128:(t + 1) * 128]
                     .rearrange("p (e r) -> p e r", e=8),
                     ps[:, 0:128].rearrange("p (e r) -> p e r", e=8),
                     2.0, wexp, ALU.mult, ALU.mult, inc=(s_ve, 1))
                psum_free[pi] = s_ve.n
                p.wait("te", s_ve, s_ve.n)
                if t == 0:
                    p.wait("te", s_ac, s_ac.n)
                p.op("te", "transpose", pX[:, t * 128:(t + 1) * 128],
                     xnst[:, t * 128:(t + 1) * 128], ident[:, :],
                     inc=(s_pe, 1) if t == 7 else None)
            p.wait("se", s_pe, s_pe.n)
            p.op("se", "activation", lwT[:, lw_off:lw_off + 1024],
                 pX[:, :], AF.Copy, inc=(s_ac, 1))
            lwT_ready = s_ac.n

            bwT_3d = bwT_d.rearrange("(k p) o -> p k o", p=128)
            wbuf_free = [s_pe.n, s_pe.n]
            for Bk in range(8):
                half = Bk % 2
                woff = half * 16384
                p.op("sy", "dma_start",
                     out=ar3[:, woff:woff + 16384]
                     .rearrange("p (k c) -> p k c", k=KCH),
                     in_=bwT_3d[:, :, 512 * Bk:512 * (Bk + 1)],
                     waits=[(s_pe, wbuf_free[half]), (s_sx, bw_ready_sx)],
                     inc=(s_lw, 16))
                lw_need = s_lw.n
                for t in range(8):
                    idx = Bk * 8 + t
                    pi = idx % 2
                    ps = pA if pi == 0 else pB
                    slot = idx % 8
                    p.wait("ge", s_ve, xslot_free[slot])
                    p.op("ge", "dma_start",
                         out=xst[:, (slot % 8) * 512:(slot % 8 + 1) * 512],
                         in_=src[res_base + 128 * t:res_base + 128 * (t + 1),
                                 512 * Bk:512 * (Bk + 1)],
                         inc=(s_lx, 16))
                    need_lx = s_lx.n
                    p.wait("te", s_lw, lw_need)
                    p.wait("te", s_ve, psum_free[pi])
                    if idx == 0:
                        p.wait("te", s_ac, lwT_ready)
                    for k in range(KCH):
                        p.op("te", "matmul", ps[:, :],
                             xn_chunk(k, t * 128, 128),
                             ar3[:, woff + k * 512:woff + (k + 1) * 512],
                             start=(k == 0), stop=False)
                    p.op("te", "matmul", ps[:, :], ones1[:, :],
                         brow[0:1, 512 * Bk:512 * (Bk + 1)],
                         start=False, stop=False)
                    p.op("te", "matmul", ps[:, :],
                         lwT[:, lw_off + t * 128:lw_off + (t + 1) * 128],
                         bflat[:, 512 * Bk:512 * (Bk + 1)],
                         start=False, stop=True, inc=(s_pe, 1))
                    if t == 7:
                        wbuf_free[half] = s_pe.n
                    eslot = idx % 2
                    p.wait("ve", s_pe, s_pe.n)
                    p.wait("ve", s_lx, need_lx)
                    p.wait("ve", s_sx, est_free[eslot])
                    p.op("ve", "scalar_tensor_tensor",
                         est[:, eslot * 512:(eslot + 1) * 512],
                         ps[:, :], 0.5, xst[:, slot * 512:(slot + 1) * 512],
                         ALU.mult, ALU.add, inc=(s_ve, 1))
                    psum_free[pi] = s_ve.n
                    xslot_free[slot] = s_ve.n
                    p.wait("se", s_ve, s_ve.n)
                    p.op("se", "dma_start",
                         out=dst[res_base + 128 * t:res_base + 128 * (t + 1),
                                 512 * Bk:512 * (Bk + 1)],
                         in_=est[:, eslot * 512:(eslot + 1) * 512],
                         inc=(s_sx, 16))
                    est_free[eslot] = s_sx.n

        def stream_to_ar3(w3d, kslice, cols, dst_off, region_free_pe):
            k0, k1 = kslice
            nk = k1 - k0
            per = max(1, 1024 // cols)
            kk = k0
            while kk < k1:
                kn = min(per, k1 - kk)
                p.op("sy", "dma_start",
                     out=wst[:, 0:kn * cols].rearrange("p (k c) -> p k c", k=kn),
                     in_=w3d[:, kk:kk + kn, :],
                     waits=[(s_gp, s_gp.n), (s_pe, region_free_pe)],
                     inc=(s_lw, 16))
                p.wait("ge", s_lw, s_lw.n)
                p.wait("ge", s_pe, region_free_pe)
                p.op("ge", "tensor_copy",
                     ar3[:, dst_off + (kk - k0) * cols:
                         dst_off + (kk - k0 + kn) * cols],
                     wst[:, 0:kn * cols], inc=(s_gp, 1))
                kk += kn
            return s_gp.n

        def proj_pass(w3d, colslice, hf, out_dram, out_rowmajor, scale):
            pass

        prologue_sx = s_sx.n
        if stage >= 1:
            ln_tiles(x_in, 0, 8)
            fence()
        if stage >= 2:
            ffn_half(x_in, x1_d, 0, 0, prologue_sx,
                     dbg_gate=(dbg['dbg_gl'], dbg['dbg_gs']) if debug else None)
            fence()
            if debug:
                p.op("ge", "dma_start", out=dbg["dbg_wf"], in_=wfull[:, :],
                     inc=(s_lx, 16))
                p.op("ge", "dma_start", out=dbg["dbg_lwT"], in_=lwT[:, :],
                     inc=(s_lx, 16))
                fence()
        if stage >= 3:
            ln_tiles(x_in, 1024, 8)
            fence()
            ffn_half(x_in, x1_d, 1024, 1024, prologue_sx)
            fence()

        wkv3d = wkv_d.rearrange("(k p) c -> p k c", p=128)
        wq3d = wq_d.rearrange("(k p) c -> p k c", p=128)
        wo3d = wo_d.rearrange("(k p) c -> p k c", p=128)

        for hf in range(2 if stage >= 4 else 0):
            ln_tiles(x1_d, hf * 1024, 8, dump=dbg.get("dbg_xn"))
            fence()
            region_pe = s_pe.n
            for h in range(H):
                stream_to_ar3(wkv3d[:, :, 128 * h:128 * (h + 1)], (0, KCH),
                              128, 0, region_pe)
                gp_need = s_gp.n
                for tq in range(2):
                    pi = tq % 2
                    ps = pA if pi == 0 else pB
                    p.wait("te", s_gp, gp_need)
                    p.wait("te", s_ac, psum_free[pi] if False else 0)
                    p.wait("te", s_ve, psum_free[pi])
                    for k in range(KCH):
                        p.op("te", "matmul", ps[:, :],
                             ar3[:, k * 128:(k + 1) * 128],
                             xn_chunk(k, tq * 512, 512),
                             start=(k == 0), stop=(k == KCH - 1),
                             inc=(s_pe, 1) if k == KCH - 1 else None)
                    region_pe = s_pe.n
                    bslot = (h * 2 + tq) % 4
                    p.wait("se", s_pe, s_pe.n)
                    p.wait("se", s_sx, bst_free[bslot])
                    p.op("se", "activation",
                         bst[:, bslot * 512:(bslot + 1) * 512], ps[:, :],
                         AF.Copy, inc=(s_ac, 1))
                    psum_free[pi] = 0
                    p.wait("te", s_ac, s_ac.n)
                    p.op("se", "dma_start",
                         out=kT_d[128 * h:128 * (h + 1),
                                  hf * 1024 + tq * 512:
                                  hf * 1024 + (tq + 1) * 512],
                         in_=bst[:, bslot * 512:(bslot + 1) * 512],
                         inc=(s_sx, 16))
                    bst_free[bslot] = s_sx.n
            fence()
            region_pe = s_pe.n
            for vb in range(8):
                stream_to_ar3(wkv3d[:, :, 4096 + 512 * vb:4096 + 512 * (vb + 1)],
                              (0, KCH), 512, 0, region_pe)
                gp_need = s_gp.n
                for t in range(8):
                    pi = t % 2
                    ps = pA if pi == 0 else pB
                    p.wait("te", s_gp, gp_need)
                    for k in range(KCH):
                        p.op("te", "matmul", ps[:, :],
                             xn_chunk(k, t * 128, 128),
                             ar3[:, k * 512:(k + 1) * 512],
                             start=(k == 0), stop=(k == KCH - 1),
                             inc=(s_pe, 1) if k == KCH - 1 else None)
                    region_pe = s_pe.n
                    bslot = t % 4
                    p.wait("se", s_pe, s_pe.n)
                    p.wait("se", s_sx, bst_free[bslot])
                    p.op("se", "activation",
                         bst[:, bslot * 512:(bslot + 1) * 512], ps[:, :],
                         AF.Copy, inc=(s_ac, 1))
                    p.wait("te", s_ac, s_ac.n)
                    p.op("se", "dma_start",
                         out=v_d[hf * 1024 + 128 * t:hf * 1024 + 128 * (t + 1),
                                 512 * vb:512 * (vb + 1)],
                         in_=bst[:, bslot * 512:(bslot + 1) * 512],
                         inc=(s_sx, 16))
                    bst_free[bslot] = s_sx.n
            fence()
            if hf == 0:
                region_pe = s_pe.n
                for h in range(H):
                    stream_to_ar3(wq3d[:, :, 192 * h:192 * h + 128], (0, KCH),
                                  128, 0, region_pe)
                    gp_need = s_gp.n
                    for tq in range(2):
                        pi = tq % 2
                        ps = pA if pi == 0 else pB
                        p.wait("te", s_gp, gp_need)
                        for k in range(KCH):
                            p.op("te", "matmul", ps[:, :],
                                 ar3[:, k * 128:(k + 1) * 128],
                                 xn_chunk(k, tq * 512, 512),
                                 start=(k == 0), stop=(k == KCH - 1),
                                 inc=(s_pe, 1) if k == KCH - 1 else None)
                        region_pe = s_pe.n
                        bslot = (h * 2 + tq) % 4
                        p.wait("se", s_pe, s_pe.n)
                        p.wait("se", s_sx, bst_free[bslot])
                        p.op("se", "activation",
                             bst[:, bslot * 512:(bslot + 1) * 512], ps[:, :],
                             AF.Copy, scale=SCALE_Q, inc=(s_ac, 1))
                        p.wait("te", s_ac, s_ac.n)
                        p.op("se", "dma_start",
                             out=qT_d[128 * h:128 * (h + 1),
                                      tq * 512:(tq + 1) * 512],
                             in_=bst[:, bslot * 512:(bslot + 1) * 512],
                             inc=(s_sx, 16))
                        bst_free[bslot] = s_sx.n
                fence()
        fence()

        H_RUN = (1 if stage == 45 else (H if stage >= 5 else 0))
        KH, VH, QH, PSB, PT = 0, 2048, 4096, 5120, 8192
        pt3 = ar3[:, PT:PT + 16384].rearrange("p (c q) -> p c q", c=16)
        vh3 = ar3[:, VH:VH + 2048].rearrange("p (c v) -> p c v", c=16)
        v3d = v_d.rearrange("(c p) v -> p c v", p=128)
        p.op("ve", "memset", stats[:, 5:6], 0.0)
        scores_done = s_pe.n
        pv_done = s_pe.n
        pt_dump_lx = [0]
        psb_dump_lx = [0]
        sc_dump_lx = [0]
        pt_evac_done = s_ac.n
        attn_evac = {0: s_ac.n, 1: s_ac.n}
        for h in range(H_RUN):
            p.wait("ge", s_pe, pv_done)
            p.op("ge", "dma_start", out=ar3[:, KH:KH + 2048],
                 in_=kT_d[128 * h:128 * (h + 1), :], inc=(s_lx, 16))
            p.op("ge", "dma_start", out=vh3,
                 in_=v3d[:, :, 128 * h:128 * (h + 1)], inc=(s_lx, 16))
            p.op("ge", "dma_start", out=ar3[:, QH:QH + 1024],
                 in_=qT_d[128 * h:128 * (h + 1), :], inc=(s_lx, 16))
            loads_done = s_lx.n
            exp_done_prev = s_ac.n
            for l in range(8):
                gg = l // 2
                kw = 256 * (gg + 1)
                p.wait("te", s_lx, loads_done)
                p.wait("te", s_ac, exp_done_prev)
                npieces = len(bank_pieces(0, kw)) * 2
                cnt = 0
                for (a, b) in bank_pieces(0, kw):
                    cnt += 1
                    p.op("te", "matmul", pS[:, a:b],
                         ar3[:, QH + 128 * l:QH + 128 * (l + 1)],
                         ar3[:, KH + a:KH + b], start=True, stop=True,
                         inc=(s_pe, 1) if cnt == npieces else None)
                for (a, b) in bank_pieces(kw, 2 * kw):
                    cnt += 1
                    p.op("te", "matmul", pS[:, a:b],
                         ar3[:, QH + 128 * l:QH + 128 * (l + 1)],
                         ar3[:, KH + 1024 + (a - kw):KH + 1024 + (b - kw)],
                         start=True, stop=True,
                         inc=(s_pe, 1) if cnt == npieces else None)
                scores_done = s_pe.n
                if False and debug and h == 0 and l == 1:
                    p.wait("se", s_lx, psb_dump_lx[0])
                if False and debug and h == 0 and l == 0:
                    p.wait("ve", s_pe, scores_done)
                    p.op("ve", "tensor_copy", xst[:, 0:2048], pS[:, 0:2048],
                         inc=(s_ve, 1))
                    p.wait("ge", s_ve, s_ve.n)
                    p.op("ge", "dma_start", out=dbg["dbg_sc"],
                         in_=xst[:, 0:2048], inc=(s_lx, 16))
                    sc_dump_lx[0] = s_lx.n
                p.wait("ve", s_pe, scores_done)
                if False and debug and h == 0 and l == 0:
                    p.wait("ve", s_lx, sc_dump_lx[0])
                p.op("ve", "reduce_max", stats[:, 5:6], pS[:, 0:2 * kw],
                     axis=AX.X, negate=True, inc=(s_ve, 1))
                p.wait("ve", s_ve, s_ve.n)
                p.op("ve", "tensor_tensor", pS[:, kw - 256:kw],
                     pS[:, kw - 256:kw],
                     masksb[:, (2 * l) * 256:(2 * l + 1) * 256], ALU.add)
                p.op("ve", "tensor_tensor", pS[:, 2 * kw - 256:2 * kw],
                     pS[:, 2 * kw - 256:2 * kw],
                     masksb[:, (2 * l + 1) * 256:(2 * l + 2) * 256], ALU.add,
                     inc=(s_ve, 1))
                p.wait("ve", s_ve, s_ve.n)
                p.op("ve", "engine_nop", inc=(s_ve, 1))
                p.wait("se", s_ve, s_ve.n)
                p.op("se", "activation", ar3[:, PSB:PSB + 2 * kw],
                     pS[:, 0:2 * kw], AF.Exp, bias=stats[:, 5:6],
                     inc=(s_ac, 1))
                exp_done_prev = s_ac.n
                p.wait("ve", s_ac, s_ac.n)
                p.op("ve", "reduce_sum", stats[:, 6:7],
                     ar3[:, PSB:PSB + 2 * kw], axis=AX.X, inc=(s_ve, 1))
                p.wait("ve", s_ve, s_ve.n)
                p.op("ve", "reciprocal", stats[:, 7:8], stats[:, 6:7],
                     inc=(s_ve, 1))
                p.wait("ve", s_ve, s_ve.n)
                p.op("ve", "tensor_scalar", ar3[:, PSB:PSB + 2 * kw],
                     ar3[:, PSB:PSB + 2 * kw], stats[:, 7:8], None, ALU.mult,
                     inc=(s_ve, 1))
                if False and debug and h == 0 and l == 0:
                    p.wait("ge", s_ve, s_ve.n)
                    p.op("ge", "dma_start", out=dbg["dbg_psb"],
                         in_=ar3[:, PSB:PSB + 2048], inc=(s_lx, 16))
                    p.op("ge", "dma_start", out=dbg["dbg_ss"],
                         in_=stats[:, :], inc=(s_lx, 16))
                    psb_dump_lx[0] = s_lx.n
                p.wait("te", s_ve, s_ve.n)
                for w in range(2):
                    nch = 2 * gg + 2
                    p.wait("te", s_ac, s_ac.n)
                    for ci in range(nch):
                        src_c = ci if w == 0 else (2 * gg + 2 + ci)
                        p.op("te", "transpose",
                             pX[:, ci * 128:(ci + 1) * 128],
                             ar3[:, PSB + src_c * 128:PSB + (src_c + 1) * 128],
                             ident[:, :],
                             inc=(s_pe, 1) if ci == nch - 1 else None)
                    p.wait("se", s_pe, s_pe.n)
                    c0 = 0 if w == 0 else 8
                    p.op("se", "activation",
                         pt3[:, c0:c0 + nch, 128 * l:128 * (l + 1)],
                         pX[:, 0:nch * 128], AF.Copy, inc=(s_ac, 1))
            pt_evac_done = s_ac.n
            if debug and h == 0:
                p.wait("ge", s_ac, pt_evac_done)
                p.op("ge", "dma_start", out=dbg["dbg_pt"],
                     in_=ar3[:, PT:PT + 16384], inc=(s_lx, 16))
                pt_dump_lx[0] = s_lx.n
            if debug and h == 1:
                p.wait("se", s_lx, pt_dump_lx[0])
            p.wait("te", s_ac, pt_evac_done)
            p.wait("te", s_ac, attn_evac[0])
            p.wait("te", s_ac, attn_evac[1])
            for c in range(16):
                lmin = 2 * ((c % 8) // 2)
                lhsT = ar3[:, VH + c * 128:VH + (c + 1) * 128]
                if lmin < 4:
                    p.op("te", "matmul", pA[:, lmin * 128:512], lhsT,
                         ar3[:, PT + c * 1024 + lmin * 128:PT + c * 1024 + 512],
                         start=(c == 0), stop=(c == 11),
                         inc=(s_pe, 1) if c == 11 else None)
                lm4 = max(lmin, 4)
                p.op("te", "matmul", pB[:, (lm4 - 4) * 128:512], lhsT,
                     ar3[:, PT + c * 1024 + lm4 * 128:PT + c * 1024 + 1024],
                     start=(c == 0), stop=(c == 15),
                     inc=(s_pe, 1) if c == 15 else None)
            pv_done = s_pe.n
            p.wait("se", s_pe, pv_done)
            p.op("se", "activation", ar1[:, h * 1024:h * 1024 + 512],
                 pA[:, :], AF.Copy, inc=(s_ac, 1))
            attn_evac[0] = s_ac.n
            p.op("se", "activation", ar1[:, h * 1024 + 512:(h + 1) * 1024],
                 pB[:, :], AF.Copy, inc=(s_ac, 1))
            attn_evac[1] = s_ac.n
        fence()
        if debug:
            p.op("ge", "dma_start", out=dbg["dbg_at"], in_=ar1[:, :],
                 inc=(s_lx, 16))
            p.op("ge", "dma_start", out=dbg["dbg_pt"],
                 in_=ar3[:, PT:PT + 16384], inc=(s_lx, 16))
            p.op("ge", "dma_start", out=dbg["dbg_psb"],
                 in_=ar3[:, PSB:PSB + 2048], inc=(s_lx, 16))
            p.op("ge", "dma_start", out=dbg["dbg_ss"],
                 in_=stats[:, :], inc=(s_lx, 16))
            fence()

        NBK_WO = 8 if stage >= 6 else 0
        psum_free = {0: 0, 1: 0}
        est_free = [s_sx.n] * 2
        xslot_free = [s_ve.n] * 8
        wbuf_free = [s_pe.n, s_pe.n]
        for Bk in range(NBK_WO):
            half = Bk % 2
            woff = half * 16384
            for pc in range(16):
                p.op("sy", "dma_start",
                     out=wst[:, :].rearrange("p (k c) -> p k c", k=2),
                     in_=wo3d[:, 2 * pc:2 * (pc + 1), 512 * Bk:512 * (Bk + 1)],
                     waits=[(s_gp, s_gp.n), (s_pe, wbuf_free[half])],
                     inc=(s_lw, 16))
                p.wait("ge", s_lw, s_lw.n)
                p.wait("ge", s_pe, wbuf_free[half])
                p.op("ge", "tensor_copy",
                     ar3[:, woff + pc * 1024:woff + (pc + 1) * 1024],
                     wst[:, :], inc=(s_gp, 1))
            gp_need = s_gp.n
            for t in range(8):
                idx = Bk * 8 + t
                pi = idx % 2
                ps = pA if pi == 0 else pB
                slot = idx % 8
                p.wait("ge", s_ve, xslot_free[slot])
                p.op("ge", "dma_start",
                     out=xst[:, slot * 512:(slot + 1) * 512],
                     in_=x1_d[128 * t:128 * (t + 1), 512 * Bk:512 * (Bk + 1)],
                     inc=(s_lx, 16))
                need_lx = s_lx.n
                p.wait("te", s_gp, gp_need)
                p.wait("te", s_ve, psum_free[pi])
                for k in range(KCH):
                    p.op("te", "matmul", ps[:, :],
                         ar1[:, k * 1024 + t * 128:k * 1024 + (t + 1) * 128],
                         ar3[:, woff + k * 512:woff + (k + 1) * 512],
                         start=(k == 0), stop=(k == KCH - 1),
                         inc=(s_pe, 1) if k == KCH - 1 else None)
                if t == 7:
                    wbuf_free[half] = s_pe.n
                eslot = idx % 2
                p.wait("ve", s_pe, s_pe.n)
                p.wait("ve", s_lx, need_lx)
                p.wait("ve", s_sx, est_free[eslot])
                p.op("ve", "scalar_tensor_tensor",
                     est[:, eslot * 512:(eslot + 1) * 512],
                     ps[:, :], 1.0, xst[:, slot * 512:(slot + 1) * 512],
                     ALU.mult, ALU.add, inc=(s_ve, 1))
                psum_free[pi] = s_ve.n
                xslot_free[slot] = s_ve.n
                p.wait("se", s_ve, s_ve.n)
                p.op("se", "dma_start",
                     out=x2_d[128 * t:128 * (t + 1), 512 * Bk:512 * (Bk + 1)],
                     in_=est[:, eslot * 512:(eslot + 1) * 512],
                     inc=(s_sx, 16))
                est_free[eslot] = s_sx.n
        fence()

        psum_free = {0: 0, 1: 0}
        if stage >= 7:
            ln_tiles(x2_d, 0, 8)
            fence()
            ffn_half(x2_d, y_d, 0, 0, prologue_sx)
            fence()

        if debug:
            p.op("ge", "dma_start", out=dbg["dbg_x1"], in_=x1_d, inc=(s_lx, 16))
            p.op("ge", "dma_start", out=dbg["dbg_x2"], in_=x2_d, inc=(s_lx, 16))
            p.op("ge", "dma_start", out=dbg["dbg_kT"], in_=kT_d, inc=(s_lx, 16))
            p.op("ge", "dma_start", out=dbg["dbg_v"], in_=v_d, inc=(s_lx, 16))
            p.op("ge", "dma_start", out=dbg["dbg_qT"], in_=qT_d, inc=(s_lx, 16))
            p.op("ge", "dma_start", out=dbg["dbg_bwT"], in_=bwT_d, inc=(s_lx, 16))
            fence()

        with nc.Block() as block:
            @block.tensor
            def _(te):
                p.run(te, "te")

            @block.vector
            def _(ve):
                p.run(ve, "ve")

            @block.scalar
            def _(se):
                p.run(se, "se")

            @block.gpsimd
            def _(ge):
                p.run(ge, "ge")

            @block.sync
            def _(sy):
                p.run(sy, "sy")

    return nc


def make_in_maps(x, wq, wkv, wo, gate, base, bias, lA, lB):
    ident = np.eye(128).astype(ml_dtypes.bfloat16)
    ones1 = np.ones((1, 128)).astype(ml_dtypes.bfloat16)
    masks = [np.ascontiguousarray(build_masks(0).reshape(16 * 128, 256)),
             np.ascontiguousarray(build_masks(1).reshape(16 * 128, 256))]
    in_maps = []
    for c in range(8):
        s, par = c // 2, c % 2
        ts = own_tiles(par) + own_tiles(1 - par)
        rows = np.concatenate([np.arange(128 * t, 128 * (t + 1)) for t in ts])
        in_maps.append({
            "x_in": np.ascontiguousarray(x[s][rows]),
            "wq": wq, "wkv": wkv, "wo": wo,
            "gate_w": gate, "base_w": base, "base_b": bias,
            "lora_A": lA.reshape(E * R, D), "lora_B": lB,
            "ident": ident, "ones1": ones1,
            "masks": masks[par],
        })
    return in_maps


def kernel(L0_attn_hidden_states, L0_attn_wq, L0_attn_wkv, L0_attn_wo,
           L0_ffn_gate_w, L0_ffn_base_w, L0_ffn_base_b, L0_ffn_lora_A,
           L0_ffn_lora_B):
    cvt = lambda a: np.ascontiguousarray(np.asarray(a, np.float32))
    x = cvt(L0_attn_hidden_states)
    in_maps = make_in_maps(
        x, cvt(L0_attn_wq), cvt(L0_attn_wkv), cvt(L0_attn_wo),
        cvt(L0_ffn_gate_w), cvt(L0_ffn_base_w),
        cvt(L0_ffn_base_b).reshape(1, D), cvt(L0_ffn_lora_A),
        cvt(L0_ffn_lora_B))
    nc = build_program(debug=False)
    res = run_bass_kernel_spmd(nc, in_maps, core_ids=list(range(8)))
    out = np.empty((4, S, D), np.float32)
    for c in range(8):
        s, par = c // 2, c % 2
        rows = np.concatenate(
            [np.arange(128 * t, 128 * (t + 1)) for t in own_tiles(par)])
        out[s][rows] = res.results[c]["y"]
    return out


# version 03
# speedup vs baseline: 1.2447x; 1.2447x over previous
import contextlib
import numpy as np
import ml_dtypes
import concourse.bass as bass
import concourse.mybir as mybir
from concourse.bass_utils import run_bass_kernel_spmd

F32 = mybir.dt.float32
BF16 = mybir.dt.bfloat16
FP8 = mybir.dt.float8e4
FP8E5 = mybir.dt.float8e5
AF = mybir.ActivationFunctionType
ALU = mybir.AluOpType
AX = mybir.AxisListType

D = 4096
S = 2048
H = 32
KCH = 32
R = 16
E = 8
SCALE_Q = float(192) ** -0.5
EPS = 1e-5
NEG = -1e30
NEG8 = -57344.0

ENGINES = ("te", "ve", "se", "ge", "sy")


class Sem:
    def __init__(self, handle):
        self.h = handle
        self.n = 0


class Prog:
    def __init__(self):
        self.q = {e: [] for e in ENGINES}

    def op(self, eng, method, *args, waits=(), inc=None, **kwargs):
        waits = tuple((s, v) for (s, v) in waits if v > 0)
        self.q[eng].append((method, args, kwargs, waits, inc))
        if inc is not None:
            inc[0].n += inc[1]

    def wait(self, eng, sem, val):
        if val > 0:
            self.q[eng].append(("wait_ge", (sem, val), {}, (), None))

    def run(self, engine_obj, eng):
        for method, args, kwargs, waits, inc in self.q[eng]:
            for sem, val in waits:
                engine_obj.wait_ge(sem.h, val)
            if method == "wait_ge":
                engine_obj.wait_ge(args[0].h, args[1])
                continue
            ins = getattr(engine_obj, method)(*args, **kwargs)
            if inc is not None:
                ins.then_inc(inc[0].h, inc[1])


def own_tiles(parity):
    return [4 * g + 2 * parity + j for g in range(4) for j in range(2)]


def build_masks(parity):
    ts = own_tiles(parity)
    pts = own_tiles(1 - parity)
    m = np.zeros((8, 2, 128, 256), np.float32)
    for l in range(8):
        g = l // 2
        tq = ts[l]
        for w in range(2):
            src = ts if w == 0 else pts
            for ci, mloc in enumerate((2 * g, 2 * g + 1)):
                tk = src[mloc]
                qpos = 128 * tq + np.arange(128)[:, None]
                kpos = 128 * tk + np.arange(128)[None, :]
                m[l, w, :, ci * 128:(ci + 1) * 128] = np.where(
                    kpos <= qpos, 0.0, NEG)
    return m.astype(ml_dtypes.bfloat16)


def bank_pieces(a, b):
    out = []
    while a < b:
        nxt = min(b, (a // 512 + 1) * 512)
        out.append((a, nxt))
        a = nxt
    return out


def build_program(debug=False, stage=99):
    nc = bass.Bass("TRN2", target_bir_lowering=False)
    p = Prog()

    x_in = nc.dram_tensor("x_in", [S, D], F32, kind="ExternalInput").ap()
    wq_d = nc.dram_tensor("wq", [D, H * 192], F32, kind="ExternalInput").ap()
    wkv_d = nc.dram_tensor("wkv", [D, 8256], F32, kind="ExternalInput").ap()
    wo_d = nc.dram_tensor("wo", [D, D], F32, kind="ExternalInput").ap()
    gate_d = nc.dram_tensor("gate_w", [E, D], F32, kind="ExternalInput").ap()
    base_d = nc.dram_tensor("base_w", [D, D], F32, kind="ExternalInput").ap()
    bias_d = nc.dram_tensor("base_b", [1, D], F32, kind="ExternalInput").ap()
    lA_d = nc.dram_tensor("lora_A", [E * R, D], F32, kind="ExternalInput").ap()
    lB_d = nc.dram_tensor("lora_B", [E, D, R], F32, kind="ExternalInput").ap()
    ident_d = nc.dram_tensor("ident", [128, 128], BF16, kind="ExternalInput").ap()
    ones_d = nc.dram_tensor("ones1", [1, 128], BF16, kind="ExternalInput").ap()
    masks_d = nc.dram_tensor("masks", [16 * 128, 256], BF16,
                             kind="ExternalInput").ap()

    y_d = nc.dram_tensor("y", [1024, D], F32, kind="ExternalOutput").ap()
    dbg = {}
    if debug:
        dbg["dbg_x1"] = nc.dram_tensor("dbg_x1", [S, D], F32, kind="ExternalOutput").ap()
        dbg["dbg_x2"] = nc.dram_tensor("dbg_x2", [1024, D], F32, kind="ExternalOutput").ap()
        dbg["dbg_kT"] = nc.dram_tensor("dbg_kT", [H * 128, S], BF16, kind="ExternalOutput").ap()
        dbg["dbg_v"] = nc.dram_tensor("dbg_v", [S, H * 128], BF16, kind="ExternalOutput").ap()
        dbg["dbg_qT"] = nc.dram_tensor("dbg_qT", [H * 128, 1024], BF16, kind="ExternalOutput").ap()
        dbg["dbg_bwT"] = nc.dram_tensor("dbg_bwT", [D, D], BF16, kind="ExternalOutput").ap()
        dbg["dbg_xn"] = nc.dram_tensor("dbg_xn", [S, D], BF16, kind="ExternalOutput").ap()
        dbg["dbg_at"] = nc.dram_tensor("dbg_at", [128, 32 * 1024], BF16, kind="ExternalOutput").ap()
        dbg["dbg_pt"] = nc.dram_tensor("dbg_pt", [128, 16 * 1024], BF16, kind="ExternalOutput").ap()
        dbg["dbg_psb"] = nc.dram_tensor("dbg_psb", [128, 2048], BF16, kind="ExternalOutput").ap()
        dbg["dbg_sc"] = nc.dram_tensor("dbg_sc", [128, 2048], F32, kind="ExternalOutput").ap()
        dbg["dbg_ss"] = nc.dram_tensor("dbg_ss", [128, 28], F32, kind="ExternalOutput").ap()
        dbg["dbg_wf"] = nc.dram_tensor("dbg_wf", [128, 64], F32, kind="ExternalOutput").ap()
        dbg["dbg_lwT"] = nc.dram_tensor("dbg_lwT", [128, 2048], BF16, kind="ExternalOutput").ap()
        dbg["dbg_gs"] = nc.dram_tensor("dbg_gs", [128, 28], F32, kind="ExternalOutput").ap()
        dbg["dbg_gl"] = nc.dram_tensor("dbg_gl", [128, 8], F32, kind="ExternalOutput").ap()

    x1_d = nc.dram_tensor("x1", [S, D], F32).ap()
    x2_d = nc.dram_tensor("x2", [1024, D], F32).ap()
    bwT_d = nc.dram_tensor("bwT", [D, D], BF16).ap()
    kT_d = nc.dram_tensor("kT", [H * 128, S], BF16).ap()
    v_d = nc.dram_tensor("v", [S, H * 128], BF16).ap()
    qT_d = nc.dram_tensor("qT", [H * 128, 1024], BF16).ap()

    ctx = contextlib.ExitStack()
    sbt = lambda name, shape, dt: ctx.enter_context(nc.sbuf_tensor(name, shape, dt))
    pst = lambda name, shape, dt: ctx.enter_context(nc.psum_tensor(name, shape, dt))
    mksem = lambda name: Sem(ctx.enter_context(nc.semaphore(name)))

    with ctx:
        s_lw = mksem("s_lw")
        s_lx = mksem("s_lx")
        s_sx = mksem("s_sx")
        s_pe = mksem("s_pe")
        s_ve = mksem("s_ve")
        s_ac = mksem("s_ac")
        s_gp = mksem("s_gp")
        ALLSEMS = (s_lw, s_lx, s_sx, s_pe, s_ve, s_ac, s_gp)

        ar1 = sbt("ar1", [128, 32 * 1024], BF16)
        ar3 = sbt("ar3", [128, 32 * 1024], BF16)
        xst = sbt("xst", [128, 4096], F32)
        wst = sbt("wst", [128, 1024], F32)
        xnst = sbt("xnst", [128, 4096], BF16)
        est = sbt("est", [128, 2 * 512], F32)
        bst = sbt("bst", [128, 4 * 512], BF16)
        junk = sbt("junk", [128, 4096], FP8)
        stats = sbt("stats", [128, 28], F32)
        bflat = sbt("bflat", [128, D], BF16)
        aflat = sbt("aflat", [128, KCH * 128], BF16)
        gwT = sbt("gwT", [128, KCH * 8], BF16)
        lwT = sbt("lwT", [128, S], BF16)
        wfull = sbt("wfull", [128, 8 * 8], F32)
        brow = sbt("brow", [1, D], BF16)
        ident = sbt("ident_s", [128, 128], BF16)
        ones1 = sbt("ones1_s", [1, 128], BF16)
        masksb = sbt("masksb_s", [128, 16 * 256], BF16)

        pS = pst("pS", [128, 2048], F32)
        pX = pst("pX", [128, 1024], BF16)
        pA = pst("pA", [128, 512], F32)
        pB = pst("pB", [128, 512], F32)
        pC = pst("pC", [128, 512], F32)

        def psq(i):
            return pS[:, 512 * i:512 * (i + 1)]

        def fence():
            for eng in ENGINES:
                for s in ALLSEMS:
                    p.wait(eng, s, s.n)

        p.op("sy", "dma_start", out=ident[:, :], in_=ident_d, inc=(s_lw, 16))
        p.op("sy", "dma_start", out=ones1[:, :], in_=ones_d, inc=(s_lw, 16))
        p.op("sy", "dma_start",
             out=masksb[:, :].rearrange("r (l k) -> r l k", l=16),
             in_=masks_d.rearrange("(l r) k -> r l k", r=128),
             inc=(s_lw, 16))
        p.op("sy", "dma_start", out=xst[0:1, 0:D], in_=bias_d, inc=(s_lw, 16))
        p.wait("ge", s_lw, s_lw.n)
        p.op("ge", "tensor_copy", brow[:, :], xst[0:1, 0:D], inc=(s_gp, 1))
        p.op("ge", "memset", stats[:, 24:25], EPS, inc=(s_gp, 1))

        for piece in range(4 if (stage >= -4 and stage != -20 and stage != -22) else 0):
            p.op("sy", "dma_start", out=wst[0:8, 0:1024],
                 in_=gate_d[:, 1024 * piece:1024 * (piece + 1)],
                 waits=[(s_gp, s_gp.n)], inc=(s_lw, 16))
            p.wait("ge", s_lw, s_lw.n)
            p.wait("ge", s_pe, s_pe.n)
            p.op("ge", "tensor_copy", bst[0:8, 0:1024], wst[0:8, 0:1024],
                 inc=(s_gp, 1))
            p.wait("te", s_gp, s_gp.n)
            p.wait("te", s_ac, s_ac.n)
            for i in range(8):
                p.op("te", "transpose", pX[:, i * 8:(i + 1) * 8],
                     bst[0:8, i * 128:(i + 1) * 128], ident[0:8, 0:8],
                     inc=(s_pe, 1) if i == 7 else None)
            p.wait("se", s_pe, s_pe.n)
            p.op("se", "activation", gwT[:, piece * 64:(piece + 1) * 64],
                 pX[:, 0:64], AF.Copy, inc=(s_ac, 1))

        for piece in range(4 if (stage >= -3 and stage not in (-20, -21)) else 0):
            p.op("sy", "dma_start", out=wst[:, 0:1024],
                 in_=lA_d[:, 1024 * piece:1024 * (piece + 1)],
                 waits=[(s_gp, s_gp.n)], inc=(s_lw, 16))
            p.wait("ge", s_lw, s_lw.n)
            p.op("ge", "tensor_copy", ar1[:, piece * 1024:(piece + 1) * 1024],
                 wst[:, 0:1024], inc=(s_gp, 1))
        for fill in range(4 if (stage >= -3 and stage not in (-20, -21)) else 0):
            p.wait("te", s_gp, s_gp.n)
            p.wait("te", s_ac, s_ac.n)
            for i in range(8):
                k = fill * 8 + i
                p.op("te", "transpose", pX[:, i * 128:(i + 1) * 128],
                     ar1[:, k * 128:(k + 1) * 128], ident[:, :],
                     inc=(s_pe, 1) if i == 7 else None)
            p.wait("se", s_pe, s_pe.n)
            p.op("se", "activation", aflat[:, fill * 1024:(fill + 1) * 1024],
                 pX[:, :], AF.Copy, inc=(s_ac, 1))

        for e in range(E if (stage >= -2 or stage in (-20, -21, -22)) else 0):
            p.op("sy", "dma_start",
                 out=wst[:, 0:512].rearrange("p (c r) -> p c r", c=KCH),
                 in_=lB_d[e].rearrange("(c p) r -> p c r", p=128),
                 waits=[(s_gp, s_gp.n)], inc=(s_lw, 16))
            p.wait("ge", s_lw, s_lw.n)
            p.wait("ge", s_pe, s_pe.n)
            p.op("ge", "tensor_copy", bst[:, 0:512], wst[:, 0:512],
                 inc=(s_gp, 1))
            for fill in range(4):
                p.wait("te", s_gp, s_gp.n)
                p.wait("te", s_ve, s_ve.n)
                p.wait("te", s_ac, s_ac.n)
                for i in range(8):
                    c = fill * 8 + i
                    p.op("te", "transpose", pX[0:16, i * 128:(i + 1) * 128],
                         bst[:, c * 16:(c + 1) * 16], ident[:, :],
                         inc=(s_pe, 1) if i == 7 else None)
                p.wait("ve", s_pe, s_pe.n)
                p.wait("ve", s_lx, s_lx.n)
                p.op("ve", "tensor_copy", bst[0:16, 1024:2048],
                     pX[0:16, 0:1024], inc=(s_ve, 1))
                p.wait("ge", s_ve, s_ve.n)
                p.op("ge", "dma_start",
                     out=bflat[16 * e:16 * (e + 1),
                               fill * 1024:(fill + 1) * 1024],
                     in_=bst[0:16, 1024:2048], inc=(s_lx, 16))
        fence()

        wst_free = [s_gp.n, s_gp.n]
        junk_free = [s_sx.n, s_sx.n]
        bst_half_pe = [0, 0]
        it = 0
        for Bk in range(8 if stage >= -1 else 0):
            for q in range(4):
                for dp in range(4):
                    half = it % 2
                    wslc = wst[:, 0:1024]
                    bslc = bst[:, half * 1024:(half + 1) * 1024]
                    jslc = xnst[:, half * 1024:(half + 1) * 1024]
                    p.op("sy", "dma_start", out=wslc,
                         in_=base_d[512 * Bk + 128 * q:512 * Bk + 128 * (q + 1),
                                    1024 * dp:1024 * (dp + 1)],
                         waits=[(s_gp, wst_free[0])], inc=(s_lw, 16))
                    p.wait("ge", s_lw, s_lw.n)
                    p.wait("ge", s_pe, bst_half_pe[half])
                    p.op("ge", "tensor_copy", bslc, wslc, inc=(s_gp, 1))
                    wst_free[0] = s_gp.n
                    p.wait("te", s_gp, s_gp.n)
                    p.wait("te", s_ac, s_ac.n)
                    for i in range(8):
                        p.op("te", "transpose", pX[:, i * 128:(i + 1) * 128],
                             bslc[:, i * 128:(i + 1) * 128], ident[:, :],
                             inc=(s_pe, 1) if i == 7 else None)
                    bst_half_pe[half] = s_pe.n
                    p.wait("se", s_pe, s_pe.n)
                    p.wait("se", s_sx, junk_free[half])
                    p.op("se", "activation", jslc, pX[:, :], AF.Copy,
                         inc=(s_ac, 1))
                    p.wait("se", s_ac, s_ac.n)
                    for i in range(8):
                        p.op("se", "dma_start",
                             out=bwT_d[1024 * dp + 128 * i:
                                       1024 * dp + 128 * (i + 1),
                                       512 * Bk + 128 * q:
                                       512 * Bk + 128 * (q + 1)],
                             in_=jslc[:, i * 128:(i + 1) * 128],
                             inc=(s_sx, 16))
                    junk_free[half] = s_sx.n
                    it += 1
        fence()

        est_free = [s_sx.n] * 2
        bst_free = [s_sx.n] * 4
        xslot_free = [s_ve.n] * 8
        psum_free = {0: s_ve.n, 1: s_ve.n}

        def ln_tiles(src, src_off, ntiles, dump=None):
            ar1_3d = ar1[:, :].rearrange("p (k c) -> p k c", k=KCH)
            prev_tr_pe = s_pe.n
            p.op("ve", "memset", stats[:, 8:9], 0.0)
            for t in range(ntiles):
                r0 = src_off + 128 * t
                p.wait("ge", s_ve, s_ve.n)
                p.op("ge", "dma_start", out=xst[:, 0:2048],
                     in_=src[r0:r0 + 128, 0:2048], inc=(s_lx, 16))
                p.op("ge", "dma_start", out=xst[:, 2048:4096],
                     in_=src[r0:r0 + 128, 2048:4096], inc=(s_lx, 16))
                p.wait("ve", s_lx, s_lx.n)
                p.op("ve", "reduce_sum", stats[:, 0:1], xst[:, :], axis=AX.X,
                     inc=(s_ve, 1))
                p.wait("ve", s_ve, s_ve.n)
                p.op("ve", "engine_nop", inc=(s_ve, 1))
                p.wait("se", s_ve, s_ve.n)
                p.op("se", "activation", stats[:, 1:2], stats[:, 0:1],
                     AF.Copy, scale=-1.0 / D, inc=(s_ac, 1))
                p.wait("ve", s_ac, s_ac.n)
                p.op("ve", "tensor_scalar", stats[:, 9:10], stats[:, 8:9],
                     0.0, 0.0, ALU.add, ALU.add, accum_out=stats[:, 10:11])
                p.op("ve", "scalar_tensor_tensor", junk[:, :], xst[:, :],
                     stats[:, 1:2], xst[:, :], ALU.add, ALU.mult,
                     accum_out=stats[:, 2:3], inc=(s_ve, 1))
                p.wait("ve", s_ve, s_ve.n)
                p.op("ve", "engine_nop", inc=(s_ve, 1))
                p.wait("se", s_ve, s_ve.n)
                p.op("se", "activation", stats[:, 3:4], stats[:, 2:3], AF.Ln,
                     scale=1.0 / D, bias=stats[:, 24:25])
                p.op("se", "activation", stats[:, 4:5], stats[:, 3:4], AF.Exp,
                     scale=-0.5, inc=(s_ac, 1))
                p.wait("ve", s_ac, s_ac.n)
                p.wait("ve", s_pe, prev_tr_pe)
                if dump is not None:
                    p.wait("ve", s_sx, s_sx.n)
                p.op("ve", "tensor_scalar", xnst[:, :], xst[:, :],
                     stats[:, 1:2], stats[:, 4:5], ALU.add, ALU.mult,
                     inc=(s_ve, 1))
                if dump is not None:
                    p.wait("se", s_ve, s_ve.n)
                    p.op("se", "dma_start",
                         out=dump[src_off + 128 * t:src_off + 128 * (t + 1), :],
                         in_=xnst[:, :], inc=(s_sx, 16))
                p.wait("te", s_ve, s_ve.n)
                for fill in range(4):
                    p.wait("te", s_ac, s_ac.n)
                    for i in range(8):
                        k = fill * 8 + i
                        p.op("te", "transpose", pX[:, i * 128:(i + 1) * 128],
                             xnst[:, k * 128:(k + 1) * 128], ident[:, :],
                             inc=(s_pe, 1) if i == 7 else None)
                    p.wait("se", s_pe, s_pe.n)
                    p.op("se", "activation",
                         ar1_3d[:, fill * 8:(fill + 1) * 8,
                                t * 128:(t + 1) * 128],
                         pX[:, :], AF.Copy, inc=(s_ac, 1))
                prev_tr_pe = s_pe.n

        def xn_chunk(k, c0, w):
            return ar1[:, k * 1024 + c0:k * 1024 + c0 + w]

        def ffn_half(src, dst, res_base, lw_off, bw_ready_sx, dbg_gate=None):
            for t in range(8):
                p.wait("te", s_ve, s_ve.n)
                for k in range(KCH):
                    p.op("te", "matmul", pC[:, 0:8],
                         xn_chunk(k, t * 128, 128), gwT[:, k * 8:(k + 1) * 8],
                         start=(k == 0), stop=(k == KCH - 1),
                         inc=(s_pe, 1) if k == KCH - 1 else None)
                p.wait("ve", s_pe, s_pe.n)
                p.op("ve", "reduce_max", stats[:, 5:6], pC[:, 0:8],
                     axis=AX.X, negate=True, inc=(s_ve, 1))
                p.wait("ve", s_ve, s_ve.n)
                p.op("ve", "tensor_scalar", stats[:, 8:16], pC[:, 0:8],
                     stats[:, 5:6], 0.0, ALU.add, ALU.is_equal, inc=(s_ve, 1))
                p.wait("ve", s_ve, s_ve.n)
                p.op("ve", "scalar_tensor_tensor", stats[:, 16:24],
                     stats[:, 8:16], NEG, pC[:, 0:8], ALU.mult, ALU.add,
                     inc=(s_ve, 1))
                p.wait("ve", s_ve, s_ve.n)
                p.op("ve", "reduce_max", stats[:, 6:7], stats[:, 16:24],
                     axis=AX.X, negate=True, inc=(s_ve, 1))
                p.wait("ve", s_ve, s_ve.n)
                p.op("ve", "tensor_scalar", stats[:, 8:16], pC[:, 0:8],
                     stats[:, 6:7], 0.0, ALU.add, ALU.is_ge, inc=(s_ve, 1))
                p.wait("ve", s_ve, s_ve.n)
                p.op("ve", "tensor_tensor", stats[:, 0:1], stats[:, 5:6],
                     stats[:, 6:7], ALU.subtract, inc=(s_ve, 1))
                p.wait("ve", s_ve, s_ve.n)
                p.op("ve", "engine_nop", inc=(s_ve, 1))
                p.wait("se", s_ve, s_ve.n)
                p.op("se", "activation", stats[:, 16:24], pC[:, 0:8], AF.Exp,
                     bias=stats[:, 5:6])
                p.op("se", "activation", stats[:, 1:2], stats[:, 0:1],
                     AF.Exp, inc=(s_ac, 1))
                p.wait("ve", s_ac, s_ac.n)
                p.op("ve", "tensor_scalar", stats[:, 2:3], stats[:, 1:2],
                     1.0, None, ALU.add, inc=(s_ve, 1))
                p.wait("ve", s_ve, s_ve.n)
                p.op("ve", "reciprocal", stats[:, 3:4], stats[:, 2:3],
                     inc=(s_ve, 1))
                p.wait("ve", s_ve, s_ve.n)
                p.op("ve", "scalar_tensor_tensor", wfull[:, t * 8:(t + 1) * 8],
                     stats[:, 16:24], stats[:, 3:4], stats[:, 8:16],
                     ALU.mult, ALU.mult, inc=(s_ve, 1))
                if dbg_gate is not None and t == 7:
                    p.op("ve", "tensor_copy", est[:, 0:8], pC[:, 0:8],
                         inc=(s_ve, 1))
                    p.wait("ge", s_ve, s_ve.n)
                    p.op("ge", "dma_start", out=dbg_gate[0], in_=est[:, 0:8],
                         inc=(s_lx, 16))
                    p.op("ge", "dma_start", out=dbg_gate[1], in_=stats[:, :],
                         inc=(s_lx, 16))
                    p.wait("te", s_lx, s_lx.n)
            for t in range(8):
                pi = t % 2
                ps = pA if pi == 0 else pB
                p.wait("te", s_ve, psum_free[pi])
                for k in range(KCH):
                    p.op("te", "matmul", ps[:, 0:128],
                         xn_chunk(k, t * 128, 128),
                         aflat[:, k * 128:(k + 1) * 128],
                         start=(k == 0), stop=(k == KCH - 1),
                         inc=(s_pe, 1) if k == KCH - 1 else None)
                wexp = wfull[:, t * 8:(t + 1) * 8].unsqueeze(2) \
                    .broadcast_to([128, 8, 16])
                p.wait("ve", s_pe, s_pe.n)
                p.op("ve", "scalar_tensor_tensor",
                     xnst[:, t * 128:(t + 1) * 128]
                     .rearrange("p (e r) -> p e r", e=8),
                     ps[:, 0:128].rearrange("p (e r) -> p e r", e=8),
                     2.0, wexp, ALU.mult, ALU.mult, inc=(s_ve, 1))
                psum_free[pi] = s_ve.n
                p.wait("te", s_ve, s_ve.n)
                if t == 0:
                    p.wait("te", s_ac, s_ac.n)
                p.op("te", "transpose", pX[:, t * 128:(t + 1) * 128],
                     xnst[:, t * 128:(t + 1) * 128], ident[:, :],
                     inc=(s_pe, 1) if t == 7 else None)
            p.wait("se", s_pe, s_pe.n)
            p.op("se", "activation", lwT[:, lw_off:lw_off + 1024],
                 pX[:, :], AF.Copy, inc=(s_ac, 1))
            lwT_ready = s_ac.n

            bwT_3d = bwT_d.rearrange("(k p) o -> p k o", p=128)
            wbuf_free = [s_pe.n, s_pe.n]
            for Bk in range(8):
                half = Bk % 2
                woff = half * 16384
                p.op("sy", "dma_start",
                     out=ar3[:, woff:woff + 16384]
                     .rearrange("p (k c) -> p k c", k=KCH),
                     in_=bwT_3d[:, :, 512 * Bk:512 * (Bk + 1)],
                     waits=[(s_pe, wbuf_free[half]), (s_sx, bw_ready_sx)],
                     inc=(s_lw, 16))
                lw_need = s_lw.n
                for t in range(8):
                    idx = Bk * 8 + t
                    pi = idx % 2
                    ps = pA if pi == 0 else pB
                    slot = idx % 8
                    p.wait("ge", s_ve, xslot_free[slot])
                    p.op("ge", "dma_start",
                         out=xst[:, (slot % 8) * 512:(slot % 8 + 1) * 512],
                         in_=src[res_base + 128 * t:res_base + 128 * (t + 1),
                                 512 * Bk:512 * (Bk + 1)],
                         inc=(s_lx, 16))
                    need_lx = s_lx.n
                    p.wait("te", s_lw, lw_need)
                    p.wait("te", s_ve, psum_free[pi])
                    if idx == 0:
                        p.wait("te", s_ac, lwT_ready)
                    for k in range(KCH):
                        p.op("te", "matmul", ps[:, :],
                             xn_chunk(k, t * 128, 128),
                             ar3[:, woff + k * 512:woff + (k + 1) * 512],
                             start=(k == 0), stop=False)
                    p.op("te", "matmul", ps[:, :], ones1[:, :],
                         brow[0:1, 512 * Bk:512 * (Bk + 1)],
                         start=False, stop=False)
                    p.op("te", "matmul", ps[:, :],
                         lwT[:, lw_off + t * 128:lw_off + (t + 1) * 128],
                         bflat[:, 512 * Bk:512 * (Bk + 1)],
                         start=False, stop=True, inc=(s_pe, 1))
                    if t == 7:
                        wbuf_free[half] = s_pe.n
                    eslot = idx % 2
                    p.wait("ve", s_pe, s_pe.n)
                    p.wait("ve", s_lx, need_lx)
                    p.wait("ve", s_sx, est_free[eslot])
                    p.op("ve", "scalar_tensor_tensor",
                         est[:, eslot * 512:(eslot + 1) * 512],
                         ps[:, :], 0.5, xst[:, slot * 512:(slot + 1) * 512],
                         ALU.mult, ALU.add, inc=(s_ve, 1))
                    psum_free[pi] = s_ve.n
                    xslot_free[slot] = s_ve.n
                    p.wait("se", s_ve, s_ve.n)
                    p.op("se", "dma_start",
                         out=dst[res_base + 128 * t:res_base + 128 * (t + 1),
                                 512 * Bk:512 * (Bk + 1)],
                         in_=est[:, eslot * 512:(eslot + 1) * 512],
                         inc=(s_sx, 16))
                    est_free[eslot] = s_sx.n

        def stream_to_ar3(w3d, kslice, cols, dst_off, region_free_pe):
            k0, k1 = kslice
            nk = k1 - k0
            per = max(1, 1024 // cols)
            kk = k0
            while kk < k1:
                kn = min(per, k1 - kk)
                p.op("sy", "dma_start",
                     out=wst[:, 0:kn * cols].rearrange("p (k c) -> p k c", k=kn),
                     in_=w3d[:, kk:kk + kn, :],
                     waits=[(s_gp, s_gp.n), (s_pe, region_free_pe)],
                     inc=(s_lw, 16))
                p.wait("ge", s_lw, s_lw.n)
                p.wait("ge", s_pe, region_free_pe)
                p.op("ge", "tensor_copy",
                     ar3[:, dst_off + (kk - k0) * cols:
                         dst_off + (kk - k0 + kn) * cols],
                     wst[:, 0:kn * cols], inc=(s_gp, 1))
                kk += kn
            return s_gp.n

        def proj_pass(w3d, colslice, hf, out_dram, out_rowmajor, scale):
            pass

        prologue_sx = s_sx.n
        if stage >= 1:
            ln_tiles(x_in, 0, 8)
            fence()
        if stage >= 2:
            ffn_half(x_in, x1_d, 0, 0, prologue_sx,
                     dbg_gate=(dbg['dbg_gl'], dbg['dbg_gs']) if debug else None)
            fence()
            if debug:
                p.op("ge", "dma_start", out=dbg["dbg_wf"], in_=wfull[:, :],
                     inc=(s_lx, 16))
                p.op("ge", "dma_start", out=dbg["dbg_lwT"], in_=lwT[:, :],
                     inc=(s_lx, 16))
                fence()
        if stage >= 3:
            ln_tiles(x_in, 1024, 8)
            fence()
            ffn_half(x_in, x1_d, 1024, 1024, prologue_sx)
            fence()

        wkv3d = wkv_d.rearrange("(k p) c -> p k c", p=128)
        wq3d = wq_d.rearrange("(k p) c -> p k c", p=128)
        wo3d = wo_d.rearrange("(k p) c -> p k c", p=128)

        for hf in range(2 if stage >= 4 else 0):
            ln_tiles(x1_d, hf * 1024, 8, dump=dbg.get("dbg_xn"))
            fence()
            reg_pe = [0, 0]
            ac_free = {0: 0, 1: 0}
            gp_need_arr = [0] * (H + 1)
            stream_to_ar3(wkv3d[:, :, 0:128], (0, KCH), 128, 0, 0)
            gp_need_arr[0] = s_gp.n
            for h in range(H):
                if h + 1 < H:
                    stream_to_ar3(wkv3d[:, :, 128 * (h + 1):128 * (h + 2)],
                                  (0, KCH), 128, ((h + 1) % 2) * 4096,
                                  reg_pe[(h + 1) % 2])
                    gp_need_arr[h + 1] = s_gp.n
                roff = (h % 2) * 4096
                for tq in range(2):
                    pi = tq % 2
                    ps = pA if pi == 0 else pB
                    p.wait("te", s_gp, gp_need_arr[h])
                    p.wait("te", s_ac, ac_free[pi])
                    for k in range(KCH):
                        p.op("te", "matmul", ps[:, :],
                             ar3[:, roff + k * 128:roff + (k + 1) * 128],
                             xn_chunk(k, tq * 512, 512),
                             start=(k == 0), stop=(k == KCH - 1),
                             inc=(s_pe, 1) if k == KCH - 1 else None)
                    reg_pe[h % 2] = s_pe.n
                    bslot = (h * 2 + tq) % 4
                    p.wait("se", s_pe, s_pe.n)
                    p.wait("se", s_sx, bst_free[bslot])
                    p.op("se", "activation",
                         bst[:, bslot * 512:(bslot + 1) * 512], ps[:, :],
                         AF.Copy, inc=(s_ac, 1))
                    ac_free[pi] = s_ac.n
                    p.op("se", "dma_start",
                         out=kT_d[128 * h:128 * (h + 1),
                                  hf * 1024 + tq * 512:
                                  hf * 1024 + (tq + 1) * 512],
                         in_=bst[:, bslot * 512:(bslot + 1) * 512],
                         inc=(s_sx, 16))
                    bst_free[bslot] = s_sx.n
            fence()
            reg_pe = [0, 0]
            ac_free = {0: 0, 1: 0}
            gp_need_arr = [0] * 9
            stream_to_ar3(wkv3d[:, :, 4096:4608], (0, KCH), 512, 0, 0)
            gp_need_arr[0] = s_gp.n
            for vb in range(8):
                if vb + 1 < 8:
                    stream_to_ar3(
                        wkv3d[:, :, 4096 + 512 * (vb + 1):4096 + 512 * (vb + 2)],
                        (0, KCH), 512, ((vb + 1) % 2) * 16384,
                        reg_pe[(vb + 1) % 2])
                    gp_need_arr[vb + 1] = s_gp.n
                roff = (vb % 2) * 16384
                for t in range(8):
                    pi = t % 2
                    ps = pA if pi == 0 else pB
                    p.wait("te", s_gp, gp_need_arr[vb])
                    p.wait("te", s_ac, ac_free[pi])
                    for k in range(KCH):
                        p.op("te", "matmul", ps[:, :],
                             xn_chunk(k, t * 128, 128),
                             ar3[:, roff + k * 512:roff + (k + 1) * 512],
                             start=(k == 0), stop=(k == KCH - 1),
                             inc=(s_pe, 1) if k == KCH - 1 else None)
                    reg_pe[vb % 2] = s_pe.n
                    bslot = t % 4
                    p.wait("se", s_pe, s_pe.n)
                    p.wait("se", s_sx, bst_free[bslot])
                    p.op("se", "activation",
                         bst[:, bslot * 512:(bslot + 1) * 512], ps[:, :],
                         AF.Copy, inc=(s_ac, 1))
                    ac_free[pi] = s_ac.n
                    p.op("se", "dma_start",
                         out=v_d[hf * 1024 + 128 * t:hf * 1024 + 128 * (t + 1),
                                 512 * vb:512 * (vb + 1)],
                         in_=bst[:, bslot * 512:(bslot + 1) * 512],
                         inc=(s_sx, 16))
                    bst_free[bslot] = s_sx.n
            fence()
            if hf == 0:
                reg_pe = [0, 0]
                ac_free = {0: 0, 1: 0}
                gp_need_arr = [0] * (H + 1)
                stream_to_ar3(wq3d[:, :, 0:128], (0, KCH), 128, 0, 0)
                gp_need_arr[0] = s_gp.n
                for h in range(H):
                    if h + 1 < H:
                        stream_to_ar3(
                            wq3d[:, :, 192 * (h + 1):192 * (h + 1) + 128],
                            (0, KCH), 128, ((h + 1) % 2) * 4096,
                            reg_pe[(h + 1) % 2])
                        gp_need_arr[h + 1] = s_gp.n
                    roff = (h % 2) * 4096
                    for tq in range(2):
                        pi = tq % 2
                        ps = pA if pi == 0 else pB
                        p.wait("te", s_gp, gp_need_arr[h])
                        p.wait("te", s_ac, ac_free[pi])
                        for k in range(KCH):
                            p.op("te", "matmul", ps[:, :],
                                 ar3[:, roff + k * 128:roff + (k + 1) * 128],
                                 xn_chunk(k, tq * 512, 512),
                                 start=(k == 0), stop=(k == KCH - 1),
                                 inc=(s_pe, 1) if k == KCH - 1 else None)
                        reg_pe[h % 2] = s_pe.n
                        bslot = (h * 2 + tq) % 4
                        p.wait("se", s_pe, s_pe.n)
                        p.wait("se", s_sx, bst_free[bslot])
                        p.op("se", "activation",
                             bst[:, bslot * 512:(bslot + 1) * 512], ps[:, :],
                             AF.Copy, scale=SCALE_Q, inc=(s_ac, 1))
                        ac_free[pi] = s_ac.n
                        p.op("se", "dma_start",
                             out=qT_d[128 * h:128 * (h + 1),
                                      tq * 512:(tq + 1) * 512],
                             in_=bst[:, bslot * 512:(bslot + 1) * 512],
                             inc=(s_sx, 16))
                        bst_free[bslot] = s_sx.n
                fence()
        fence()

        H_RUN = (1 if stage == 45 else (H if stage >= 5 else 0))
        PSB, PT = 10240, 12288
        pt3 = ar3[:, PT:PT + 16384].rearrange("p (c q) -> p c q", c=16)
        v3d = v_d.rearrange("(c p) v -> p c v", p=128)
        p.op("ve", "memset", stats[:, 5:6], 0.0)
        scores_done = s_pe.n
        pv_done = s_pe.n
        pt_dump_lx = [0]
        psb_dump_lx = [0]
        sc_dump_lx = [0]
        pt_evac_done = s_ac.n
        attn_evac = {0: s_ac.n, 1: s_ac.n}
        pv_done_arr = [0] * (H + 1)
        loads_arr = [0] * (H + 1)

        def attn_loads(hh, wait_pe):
            i = hh % 2
            p.wait("ge", s_pe, wait_pe)
            p.op("ge", "dma_start", out=ar3[:, i * 2048:i * 2048 + 2048],
                 in_=kT_d[128 * hh:128 * (hh + 1), :], inc=(s_lx, 16))
            p.op("ge", "dma_start",
                 out=ar3[:, 4096 + i * 2048:4096 + i * 2048 + 2048]
                 .rearrange("p (c v) -> p c v", c=16),
                 in_=v3d[:, :, 128 * hh:128 * (hh + 1)], inc=(s_lx, 16))
            p.op("ge", "dma_start",
                 out=ar3[:, 8192 + i * 1024:8192 + i * 1024 + 1024],
                 in_=qT_d[128 * hh:128 * (hh + 1), :], inc=(s_lx, 16))
            loads_arr[hh] = s_lx.n

        if H_RUN > 0:
            attn_loads(0, 0)
        for h in range(H_RUN):
            KH = (h % 2) * 2048
            VH = 4096 + (h % 2) * 2048
            QH = 8192 + (h % 2) * 1024
            if h + 1 < H_RUN:
                attn_loads(h + 1, pv_done_arr[h - 1] if h >= 1 else 0)
            loads_done = loads_arr[h]
            exp_done_prev = s_ac.n

            def emit_scores(l):
                gg = l // 2
                kw = 256 * (gg + 1)
                p.wait("te", s_lx, loads_done)
                p.wait("te", s_ac, exp_done_prev)
                pieces = []
                for (a, b) in bank_pieces(0, kw):
                    pieces.append((a, b, ar3[:, KH + a:KH + b]))
                for (a, b) in bank_pieces(kw, 2 * kw):
                    pieces.append((a, b,
                                   ar3[:, KH + 1024 + (a - kw):
                                       KH + 1024 + (b - kw)]))
                banks = set()
                for (a, b, rhs) in pieces:
                    p.op("te", "matmul", pS[:, a:b],
                         ar3[:, QH + 128 * l:QH + 128 * (l + 1)], rhs,
                         start=(a // 512 not in banks), stop=True)
                    banks.add(a // 512)
                for w, (ma, mb) in enumerate(((kw - 256, kw),
                                              (2 * kw - 256, 2 * kw))):
                    p.op("te", "matmul", pS[:, ma:mb], ident[:, :],
                         masksb[:, (2 * l + w) * 256:(2 * l + w + 1) * 256],
                         start=False, stop=True, skip_group_check=True,
                         inc=(s_pe, 1) if w == 1 else None)

            emit_scores(0)
            for l in range(8):
                gg = l // 2
                kw = 256 * (gg + 1)
                p.wait("ve", s_pe, s_pe.n)
                p.op("ve", "reduce_max", stats[:, 5:6], pS[:, 0:2 * kw],
                     axis=AX.X, negate=True, inc=(s_ve, 1))
                p.wait("ve", s_ve, s_ve.n)
                p.op("ve", "engine_nop", inc=(s_ve, 1))
                p.wait("se", s_ve, s_ve.n)
                p.op("se", "activation", ar3[:, PSB:PSB + 2 * kw],
                     pS[:, 0:2 * kw], AF.Exp, bias=stats[:, 5:6],
                     inc=(s_ac, 1))
                exp_done_prev = s_ac.n
                p.wait("ve", s_ac, s_ac.n)
                p.op("ve", "reduce_sum", stats[:, 6:7],
                     ar3[:, PSB:PSB + 2 * kw], axis=AX.X, inc=(s_ve, 1))
                p.wait("ve", s_ve, s_ve.n)
                p.op("ve", "reciprocal", stats[:, 7:8], stats[:, 6:7],
                     inc=(s_ve, 1))
                p.wait("ve", s_ve, s_ve.n)
                p.op("ve", "tensor_scalar", ar3[:, PSB:PSB + 2 * kw],
                     ar3[:, PSB:PSB + 2 * kw], stats[:, 7:8], None, ALU.mult,
                     inc=(s_ve, 1))
                scale_done = s_ve.n
                if l < 7:
                    emit_scores(l + 1)
                p.wait("te", s_ve, scale_done)
                for w in range(2):
                    nch = 2 * gg + 2
                    p.wait("te", s_ac, s_ac.n)
                    for ci in range(nch):
                        src_c = ci if w == 0 else (2 * gg + 2 + ci)
                        p.op("te", "transpose",
                             pX[:, ci * 128:(ci + 1) * 128],
                             ar3[:, PSB + src_c * 128:PSB + (src_c + 1) * 128],
                             ident[:, :],
                             inc=(s_pe, 1) if ci == nch - 1 else None)
                    p.wait("se", s_pe, s_pe.n)
                    c0 = 0 if w == 0 else 8
                    p.op("se", "activation",
                         pt3[:, c0:c0 + nch, 128 * l:128 * (l + 1)],
                         pX[:, 0:nch * 128], AF.Copy, inc=(s_ac, 1))
            pt_evac_done = s_ac.n
            if debug and h == 0:
                p.wait("ge", s_ac, pt_evac_done)
                p.op("ge", "dma_start", out=dbg["dbg_pt"],
                     in_=ar3[:, PT:PT + 16384], inc=(s_lx, 16))
                pt_dump_lx[0] = s_lx.n
            if debug and h == 1:
                p.wait("se", s_lx, pt_dump_lx[0])
            p.wait("te", s_ac, pt_evac_done)
            p.wait("te", s_ac, attn_evac[0])
            p.wait("te", s_ac, attn_evac[1])
            for c in range(16):
                lmin = 2 * ((c % 8) // 2)
                lhsT = ar3[:, VH + c * 128:VH + (c + 1) * 128]
                if lmin < 4:
                    p.op("te", "matmul", pA[:, lmin * 128:512], lhsT,
                         ar3[:, PT + c * 1024 + lmin * 128:PT + c * 1024 + 512],
                         start=(c == 0), stop=(c == 11),
                         inc=(s_pe, 1) if c == 11 else None)
                lm4 = max(lmin, 4)
                p.op("te", "matmul", pB[:, (lm4 - 4) * 128:512], lhsT,
                     ar3[:, PT + c * 1024 + lm4 * 128:PT + c * 1024 + 1024],
                     start=(c == 0), stop=(c == 15),
                     inc=(s_pe, 1) if c == 15 else None)
            pv_done = s_pe.n
            pv_done_arr[h] = pv_done
            p.wait("se", s_pe, pv_done)
            p.op("se", "activation", ar1[:, h * 1024:h * 1024 + 512],
                 pA[:, :], AF.Copy, inc=(s_ac, 1))
            attn_evac[0] = s_ac.n
            p.op("se", "activation", ar1[:, h * 1024 + 512:(h + 1) * 1024],
                 pB[:, :], AF.Copy, inc=(s_ac, 1))
            attn_evac[1] = s_ac.n
        fence()
        if debug:
            p.op("ge", "dma_start", out=dbg["dbg_at"], in_=ar1[:, :],
                 inc=(s_lx, 16))
            p.op("ge", "dma_start", out=dbg["dbg_pt"],
                 in_=ar3[:, PT:PT + 16384], inc=(s_lx, 16))
            p.op("ge", "dma_start", out=dbg["dbg_psb"],
                 in_=ar3[:, PSB:PSB + 2048], inc=(s_lx, 16))
            p.op("ge", "dma_start", out=dbg["dbg_ss"],
                 in_=stats[:, :], inc=(s_lx, 16))
            fence()

        NBK_WO = 8 if stage >= 6 else 0
        psum_free = {0: 0, 1: 0}
        est_free = [s_sx.n] * 2
        xslot_free = [s_ve.n] * 8
        wbuf_free = [s_pe.n, s_pe.n]
        for Bk in range(NBK_WO):
            half = Bk % 2
            woff = half * 16384
            for pc in range(16):
                p.op("sy", "dma_start",
                     out=wst[:, :].rearrange("p (k c) -> p k c", k=2),
                     in_=wo3d[:, 2 * pc:2 * (pc + 1), 512 * Bk:512 * (Bk + 1)],
                     waits=[(s_gp, s_gp.n), (s_pe, wbuf_free[half])],
                     inc=(s_lw, 16))
                p.wait("ge", s_lw, s_lw.n)
                p.wait("ge", s_pe, wbuf_free[half])
                p.op("ge", "tensor_copy",
                     ar3[:, woff + pc * 1024:woff + (pc + 1) * 1024],
                     wst[:, :], inc=(s_gp, 1))
            gp_need = s_gp.n
            for t in range(8):
                idx = Bk * 8 + t
                pi = idx % 2
                ps = pA if pi == 0 else pB
                slot = idx % 8
                p.wait("ge", s_ve, xslot_free[slot])
                p.op("ge", "dma_start",
                     out=xst[:, slot * 512:(slot + 1) * 512],
                     in_=x1_d[128 * t:128 * (t + 1), 512 * Bk:512 * (Bk + 1)],
                     inc=(s_lx, 16))
                need_lx = s_lx.n
                p.wait("te", s_gp, gp_need)
                p.wait("te", s_ve, psum_free[pi])
                for k in range(KCH):
                    p.op("te", "matmul", ps[:, :],
                         ar1[:, k * 1024 + t * 128:k * 1024 + (t + 1) * 128],
                         ar3[:, woff + k * 512:woff + (k + 1) * 512],
                         start=(k == 0), stop=(k == KCH - 1),
                         inc=(s_pe, 1) if k == KCH - 1 else None)
                if t == 7:
                    wbuf_free[half] = s_pe.n
                eslot = idx % 2
                p.wait("ve", s_pe, s_pe.n)
                p.wait("ve", s_lx, need_lx)
                p.wait("ve", s_sx, est_free[eslot])
                p.op("ve", "scalar_tensor_tensor",
                     est[:, eslot * 512:(eslot + 1) * 512],
                     ps[:, :], 1.0, xst[:, slot * 512:(slot + 1) * 512],
                     ALU.mult, ALU.add, inc=(s_ve, 1))
                psum_free[pi] = s_ve.n
                xslot_free[slot] = s_ve.n
                p.wait("se", s_ve, s_ve.n)
                p.op("se", "dma_start",
                     out=x2_d[128 * t:128 * (t + 1), 512 * Bk:512 * (Bk + 1)],
                     in_=est[:, eslot * 512:(eslot + 1) * 512],
                     inc=(s_sx, 16))
                est_free[eslot] = s_sx.n
        fence()

        psum_free = {0: 0, 1: 0}
        if stage >= 7:
            ln_tiles(x2_d, 0, 8)
            fence()
            ffn_half(x2_d, y_d, 0, 0, prologue_sx)
            fence()

        if debug:
            p.op("ge", "dma_start", out=dbg["dbg_x1"], in_=x1_d, inc=(s_lx, 16))
            p.op("ge", "dma_start", out=dbg["dbg_x2"], in_=x2_d, inc=(s_lx, 16))
            p.op("ge", "dma_start", out=dbg["dbg_kT"], in_=kT_d, inc=(s_lx, 16))
            p.op("ge", "dma_start", out=dbg["dbg_v"], in_=v_d, inc=(s_lx, 16))
            p.op("ge", "dma_start", out=dbg["dbg_qT"], in_=qT_d, inc=(s_lx, 16))
            p.op("ge", "dma_start", out=dbg["dbg_bwT"], in_=bwT_d, inc=(s_lx, 16))
            fence()

        with nc.Block() as block:
            @block.tensor
            def _(te):
                p.run(te, "te")

            @block.vector
            def _(ve):
                p.run(ve, "ve")

            @block.scalar
            def _(se):
                p.run(se, "se")

            @block.gpsimd
            def _(ge):
                p.run(ge, "ge")

            @block.sync
            def _(sy):
                p.run(sy, "sy")

    return nc


def make_in_maps(x, wq, wkv, wo, gate, base, bias, lA, lB):
    ident = np.eye(128).astype(ml_dtypes.bfloat16)
    ones1 = np.ones((1, 128)).astype(ml_dtypes.bfloat16)
    masks = [np.ascontiguousarray(build_masks(0).reshape(16 * 128, 256)),
             np.ascontiguousarray(build_masks(1).reshape(16 * 128, 256))]
    in_maps = []
    for c in range(8):
        s, par = c // 2, c % 2
        ts = own_tiles(par) + own_tiles(1 - par)
        rows = np.concatenate([np.arange(128 * t, 128 * (t + 1)) for t in ts])
        in_maps.append({
            "x_in": np.ascontiguousarray(x[s][rows]),
            "wq": wq, "wkv": wkv, "wo": wo,
            "gate_w": gate, "base_w": base, "base_b": bias,
            "lora_A": lA.reshape(E * R, D), "lora_B": lB,
            "ident": ident, "ones1": ones1,
            "masks": masks[par],
        })
    return in_maps


def kernel(L0_attn_hidden_states, L0_attn_wq, L0_attn_wkv, L0_attn_wo,
           L0_ffn_gate_w, L0_ffn_base_w, L0_ffn_base_b, L0_ffn_lora_A,
           L0_ffn_lora_B):
    cvt = lambda a: np.ascontiguousarray(np.asarray(a, np.float32))
    x = cvt(L0_attn_hidden_states)
    in_maps = make_in_maps(
        x, cvt(L0_attn_wq), cvt(L0_attn_wkv), cvt(L0_attn_wo),
        cvt(L0_ffn_gate_w), cvt(L0_ffn_base_w),
        cvt(L0_ffn_base_b).reshape(1, D), cvt(L0_ffn_lora_A),
        cvt(L0_ffn_lora_B))
    nc = build_program(debug=False)
    res = run_bass_kernel_spmd(nc, in_maps, core_ids=list(range(8)))
    out = np.empty((4, S, D), np.float32)
    for c in range(8):
        s, par = c // 2, c % 2
        rows = np.concatenate(
            [np.arange(128 * t, 128 * (t + 1)) for t in own_tiles(par)])
        out[s][rows] = res.results[c]["y"]
    return out


# version 04 (final)
# speedup vs baseline: 1.5751x; 1.2654x over previous
import contextlib
import numpy as np
import ml_dtypes
import concourse.bass as bass
import concourse.mybir as mybir
from concourse.bass_utils import run_bass_kernel_spmd

F32 = mybir.dt.float32
BF16 = mybir.dt.bfloat16
FP8 = mybir.dt.float8e4
FP8E5 = mybir.dt.float8e5
AF = mybir.ActivationFunctionType
ALU = mybir.AluOpType
AX = mybir.AxisListType

D = 4096
S = 2048
H = 32
KCH = 32
R = 16
E = 8
SCALE_Q = float(192) ** -0.5
EPS = 1e-5
NEG = -1e30
NEG8 = -57344.0

ENGINES = ("te", "ve", "se", "ge", "sy")


class Sem:
    def __init__(self, handle):
        self.h = handle
        self.n = 0


class Prog:
    def __init__(self):
        self.q = {e: [] for e in ENGINES}

    def op(self, eng, method, *args, waits=(), inc=None, **kwargs):
        waits = tuple((s, v) for (s, v) in waits if v > 0)
        self.q[eng].append((method, args, kwargs, waits, inc))
        if inc is not None:
            inc[0].n += inc[1]

    def wait(self, eng, sem, val):
        if val > 0:
            self.q[eng].append(("wait_ge", (sem, val), {}, (), None))

    def run(self, engine_obj, eng):
        for method, args, kwargs, waits, inc in self.q[eng]:
            for sem, val in waits:
                engine_obj.wait_ge(sem.h, val)
            if method == "wait_ge":
                engine_obj.wait_ge(args[0].h, args[1])
                continue
            ins = getattr(engine_obj, method)(*args, **kwargs)
            if inc is not None:
                ins.then_inc(inc[0].h, inc[1])


def own_tiles(parity):
    return [4 * g + 2 * parity + j for g in range(4) for j in range(2)]


def build_masks(parity):
    ts = own_tiles(parity)
    pts = own_tiles(1 - parity)
    m = np.zeros((8, 2, 128, 256), np.float32)
    for l in range(8):
        g = l // 2
        tq = ts[l]
        for w in range(2):
            src = ts if w == 0 else pts
            for ci, mloc in enumerate((2 * g, 2 * g + 1)):
                tk = src[mloc]
                qpos = 128 * tq + np.arange(128)[:, None]
                kpos = 128 * tk + np.arange(128)[None, :]
                m[l, w, :, ci * 128:(ci + 1) * 128] = np.where(
                    kpos <= qpos, 0.0, NEG)
    return m.astype(ml_dtypes.bfloat16)


def bank_pieces(a, b):
    out = []
    while a < b:
        nxt = min(b, (a // 512 + 1) * 512)
        out.append((a, nxt))
        a = nxt
    return out


def build_program(debug=False, stage=99):
    nc = bass.Bass("TRN2", target_bir_lowering=False)
    p = Prog()

    x_in = nc.dram_tensor("x_in", [S, D], F32, kind="ExternalInput").ap()
    wq_d = nc.dram_tensor("wq", [D, H * 192], F32, kind="ExternalInput").ap()
    wkv_d = nc.dram_tensor("wkv", [D, 8256], F32, kind="ExternalInput").ap()
    wo_d = nc.dram_tensor("wo", [D, D], F32, kind="ExternalInput").ap()
    gate_d = nc.dram_tensor("gate_w", [E, D], F32, kind="ExternalInput").ap()
    base_d = nc.dram_tensor("base_w", [D, D], F32, kind="ExternalInput").ap()
    bias_d = nc.dram_tensor("base_b", [1, D], F32, kind="ExternalInput").ap()
    lA_d = nc.dram_tensor("lora_A", [E * R, D], F32, kind="ExternalInput").ap()
    lB_d = nc.dram_tensor("lora_B", [E, D, R], F32, kind="ExternalInput").ap()
    ident_d = nc.dram_tensor("ident", [128, 128], BF16, kind="ExternalInput").ap()
    ones_d = nc.dram_tensor("ones1", [1, 128], BF16, kind="ExternalInput").ap()
    masks_d = nc.dram_tensor("masks", [16 * 128, 256], BF16,
                             kind="ExternalInput").ap()

    y_d = nc.dram_tensor("y", [1024, D], F32, kind="ExternalOutput").ap()
    dbg = {}
    if debug:
        dbg["dbg_x1"] = nc.dram_tensor("dbg_x1", [S, D], F32, kind="ExternalOutput").ap()
        dbg["dbg_x2"] = nc.dram_tensor("dbg_x2", [1024, D], F32, kind="ExternalOutput").ap()
        dbg["dbg_kT"] = nc.dram_tensor("dbg_kT", [H * 128, S], BF16, kind="ExternalOutput").ap()
        dbg["dbg_v"] = nc.dram_tensor("dbg_v", [S, H * 128], BF16, kind="ExternalOutput").ap()
        dbg["dbg_qT"] = nc.dram_tensor("dbg_qT", [H * 128, 1024], BF16, kind="ExternalOutput").ap()
        dbg["dbg_bwT"] = nc.dram_tensor("dbg_bwT", [D, D], BF16, kind="ExternalOutput").ap()
        dbg["dbg_xn"] = nc.dram_tensor("dbg_xn", [S, D], BF16, kind="ExternalOutput").ap()
        dbg["dbg_at"] = nc.dram_tensor("dbg_at", [128, 32 * 1024], BF16, kind="ExternalOutput").ap()
        dbg["dbg_pt"] = nc.dram_tensor("dbg_pt", [128, 16 * 1024], BF16, kind="ExternalOutput").ap()
        dbg["dbg_psb"] = nc.dram_tensor("dbg_psb", [128, 2048], BF16, kind="ExternalOutput").ap()
        dbg["dbg_sc"] = nc.dram_tensor("dbg_sc", [128, 2048], F32, kind="ExternalOutput").ap()
        dbg["dbg_ss"] = nc.dram_tensor("dbg_ss", [128, 28], F32, kind="ExternalOutput").ap()
        dbg["dbg_wf"] = nc.dram_tensor("dbg_wf", [128, 64], F32, kind="ExternalOutput").ap()
        dbg["dbg_lwT"] = nc.dram_tensor("dbg_lwT", [128, 2048], BF16, kind="ExternalOutput").ap()
        dbg["dbg_gs"] = nc.dram_tensor("dbg_gs", [128, 28], F32, kind="ExternalOutput").ap()
        dbg["dbg_gl"] = nc.dram_tensor("dbg_gl", [128, 8], F32, kind="ExternalOutput").ap()

    x1_d = nc.dram_tensor("x1", [S, D], F32).ap()
    x2_d = nc.dram_tensor("x2", [1024, D], F32).ap()
    bwT_d = nc.dram_tensor("bwT", [D, D], BF16).ap()
    kT_d = nc.dram_tensor("kT", [H * 128, S], BF16).ap()
    v_d = nc.dram_tensor("v", [S, H * 128], BF16).ap()
    qT_d = nc.dram_tensor("qT", [H * 128, 1024], BF16).ap()

    ctx = contextlib.ExitStack()
    sbt = lambda name, shape, dt: ctx.enter_context(nc.sbuf_tensor(name, shape, dt))
    pst = lambda name, shape, dt: ctx.enter_context(nc.psum_tensor(name, shape, dt))
    mksem = lambda name: Sem(ctx.enter_context(nc.semaphore(name)))

    with ctx:
        s_lw = mksem("s_lw")
        s_lx = mksem("s_lx")
        s_sx = mksem("s_sx")
        s_pe = mksem("s_pe")
        s_ve = mksem("s_ve")
        s_ac = mksem("s_ac")
        s_gp = mksem("s_gp")
        ALLSEMS = (s_lw, s_lx, s_sx, s_pe, s_ve, s_ac, s_gp)

        ar1 = sbt("ar1", [128, 32 * 1024], BF16)
        ar3 = sbt("ar3", [128, 32 * 1024], BF16)
        xst = sbt("xst", [128, 4096], F32)
        wst = sbt("wst", [128, 1024], F32)
        xnst = sbt("xnst", [128, 4096], BF16)
        est = sbt("est", [128, 2 * 512], F32)
        bst = sbt("bst", [128, 4 * 512], BF16)
        junk = sbt("junk", [128, 4096], FP8)
        wst2 = junk[:, :].bitcast(F32)
        stats = sbt("stats", [128, 28], F32)
        bflat = sbt("bflat", [128, D], BF16)
        aflat = sbt("aflat", [128, KCH * 128], BF16)
        gwT = sbt("gwT", [128, KCH * 8], BF16)
        lwT = sbt("lwT", [128, S], BF16)
        wfull = sbt("wfull", [128, 8 * 8], F32)
        brow = sbt("brow", [1, D], BF16)
        ident = sbt("ident_s", [128, 128], BF16)
        ones1 = sbt("ones1_s", [1, 128], BF16)
        masksb = sbt("masksb_s", [128, 16 * 256], BF16)

        pS = pst("pS", [128, 2048], F32)
        pX = pst("pX", [128, 1024], BF16)
        pA = pst("pA", [128, 512], F32)
        pB = pst("pB", [128, 512], F32)
        pC = pst("pC", [128, 512], F32)

        def psq(i):
            return pS[:, 512 * i:512 * (i + 1)]

        def fence():
            for eng in ENGINES:
                for s in ALLSEMS:
                    p.wait(eng, s, s.n)

        p.op("sy", "dma_start", out=ident[:, :], in_=ident_d, inc=(s_lw, 16))
        p.op("sy", "dma_start", out=ones1[:, :], in_=ones_d, inc=(s_lw, 16))
        p.op("sy", "dma_start",
             out=masksb[:, :].rearrange("r (l k) -> r l k", l=16),
             in_=masks_d.rearrange("(l r) k -> r l k", r=128),
             inc=(s_lw, 16))
        p.op("sy", "dma_start", out=xst[0:1, 0:D], in_=bias_d, inc=(s_lw, 16))
        p.wait("ge", s_lw, s_lw.n)
        p.op("ge", "tensor_copy", brow[:, :], xst[0:1, 0:D], inc=(s_gp, 1))
        p.op("ge", "memset", stats[:, 24:25], EPS, inc=(s_gp, 1))

        for piece in range(4 if (stage >= -4 and stage != -20 and stage != -22) else 0):
            p.op("sy", "dma_start", out=wst[0:8, 0:1024],
                 in_=gate_d[:, 1024 * piece:1024 * (piece + 1)],
                 waits=[(s_gp, s_gp.n)], inc=(s_lw, 16))
            p.wait("ge", s_lw, s_lw.n)
            p.wait("ge", s_pe, s_pe.n)
            p.op("ge", "tensor_copy", bst[0:8, 0:1024], wst[0:8, 0:1024],
                 inc=(s_gp, 1))
            p.wait("te", s_gp, s_gp.n)
            p.wait("te", s_ac, s_ac.n)
            for i in range(8):
                p.op("te", "transpose", pX[:, i * 8:(i + 1) * 8],
                     bst[0:8, i * 128:(i + 1) * 128], ident[0:8, 0:8],
                     inc=(s_pe, 1) if i == 7 else None)
            p.wait("se", s_pe, s_pe.n)
            p.op("se", "activation", gwT[:, piece * 64:(piece + 1) * 64],
                 pX[:, 0:64], AF.Copy, inc=(s_ac, 1))

        for piece in range(4 if (stage >= -3 and stage not in (-20, -21)) else 0):
            p.op("sy", "dma_start", out=wst[:, 0:1024],
                 in_=lA_d[:, 1024 * piece:1024 * (piece + 1)],
                 waits=[(s_gp, s_gp.n)], inc=(s_lw, 16))
            p.wait("ge", s_lw, s_lw.n)
            p.op("ge", "tensor_copy", ar1[:, piece * 1024:(piece + 1) * 1024],
                 wst[:, 0:1024], inc=(s_gp, 1))
        for fill in range(4 if (stage >= -3 and stage not in (-20, -21)) else 0):
            p.wait("te", s_gp, s_gp.n)
            p.wait("te", s_ac, s_ac.n)
            for i in range(8):
                k = fill * 8 + i
                p.op("te", "transpose", pX[:, i * 128:(i + 1) * 128],
                     ar1[:, k * 128:(k + 1) * 128], ident[:, :],
                     inc=(s_pe, 1) if i == 7 else None)
            p.wait("se", s_pe, s_pe.n)
            p.op("se", "activation", aflat[:, fill * 1024:(fill + 1) * 1024],
                 pX[:, :], AF.Copy, inc=(s_ac, 1))

        for e in range(E if (stage >= -2 or stage in (-20, -21, -22)) else 0):
            p.op("sy", "dma_start",
                 out=wst[:, 0:512].rearrange("p (c r) -> p c r", c=KCH),
                 in_=lB_d[e].rearrange("(c p) r -> p c r", p=128),
                 waits=[(s_gp, s_gp.n)], inc=(s_lw, 16))
            p.wait("ge", s_lw, s_lw.n)
            p.wait("ge", s_pe, s_pe.n)
            p.op("ge", "tensor_copy", bst[:, 0:512], wst[:, 0:512],
                 inc=(s_gp, 1))
            for fill in range(4):
                p.wait("te", s_gp, s_gp.n)
                p.wait("te", s_ve, s_ve.n)
                p.wait("te", s_ac, s_ac.n)
                for i in range(8):
                    c = fill * 8 + i
                    p.op("te", "transpose", pX[0:16, i * 128:(i + 1) * 128],
                         bst[:, c * 16:(c + 1) * 16], ident[:, :],
                         inc=(s_pe, 1) if i == 7 else None)
                p.wait("ve", s_pe, s_pe.n)
                p.wait("ve", s_lx, s_lx.n)
                p.op("ve", "tensor_copy", bst[0:16, 1024:2048],
                     pX[0:16, 0:1024], inc=(s_ve, 1))
                p.wait("ge", s_ve, s_ve.n)
                p.op("ge", "dma_start",
                     out=bflat[16 * e:16 * (e + 1),
                               fill * 1024:(fill + 1) * 1024],
                     in_=bst[0:16, 1024:2048], inc=(s_lx, 16))
        fence()

        wst_free = [s_gp.n, s_gp.n]
        junk_free = [s_sx.n, s_sx.n]
        bst_half_pe = [0, 0]
        it = 0
        for Bk in range(8 if stage >= -1 else 0):
            for q in range(4):
                for dp in range(4):
                    half = it % 2
                    wslc = wst[:, 0:1024]
                    bslc = bst[:, half * 1024:(half + 1) * 1024]
                    jslc = xnst[:, half * 1024:(half + 1) * 1024]
                    p.op("sy", "dma_start", out=wslc,
                         in_=base_d[512 * Bk + 128 * q:512 * Bk + 128 * (q + 1),
                                    1024 * dp:1024 * (dp + 1)],
                         waits=[(s_gp, wst_free[0])], inc=(s_lw, 16))
                    p.wait("ge", s_lw, s_lw.n)
                    p.wait("ge", s_pe, bst_half_pe[half])
                    p.op("ge", "tensor_copy", bslc, wslc, inc=(s_gp, 1))
                    wst_free[0] = s_gp.n
                    p.wait("te", s_gp, s_gp.n)
                    p.wait("te", s_ac, s_ac.n)
                    for i in range(8):
                        p.op("te", "transpose", pX[:, i * 128:(i + 1) * 128],
                             bslc[:, i * 128:(i + 1) * 128], ident[:, :],
                             inc=(s_pe, 1) if i == 7 else None)
                    bst_half_pe[half] = s_pe.n
                    p.wait("se", s_pe, s_pe.n)
                    p.wait("se", s_sx, junk_free[half])
                    p.op("se", "activation", jslc, pX[:, :], AF.Copy,
                         inc=(s_ac, 1))
                    p.wait("se", s_ac, s_ac.n)
                    for i in range(8):
                        p.op("se", "dma_start",
                             out=bwT_d[1024 * dp + 128 * i:
                                       1024 * dp + 128 * (i + 1),
                                       512 * Bk + 128 * q:
                                       512 * Bk + 128 * (q + 1)],
                             in_=jslc[:, i * 128:(i + 1) * 128],
                             inc=(s_sx, 16))
                    junk_free[half] = s_sx.n
                    it += 1
        fence()

        est_free = [s_sx.n] * 2
        bst_free = [s_sx.n] * 4
        xslot_free = [s_ve.n] * 8
        psum_free = {0: s_ve.n, 1: s_ve.n}

        def ln_tiles(src, src_off, ntiles, dump=None):
            ar1_3d = ar1[:, :].rearrange("p (k c) -> p k c", k=KCH)
            prev_tr_pe = s_pe.n
            p.op("ve", "memset", stats[:, 8:9], 0.0)
            for t in range(ntiles):
                r0 = src_off + 128 * t
                p.wait("ge", s_ve, s_ve.n)
                p.op("ge", "dma_start", out=xst[:, 0:2048],
                     in_=src[r0:r0 + 128, 0:2048], inc=(s_lx, 16))
                p.op("ge", "dma_start", out=xst[:, 2048:4096],
                     in_=src[r0:r0 + 128, 2048:4096], inc=(s_lx, 16))
                p.wait("ve", s_lx, s_lx.n)
                p.op("ve", "reduce_sum", stats[:, 0:1], xst[:, :], axis=AX.X,
                     inc=(s_ve, 1))
                p.wait("ve", s_ve, s_ve.n)
                p.op("ve", "engine_nop", inc=(s_ve, 1))
                p.wait("se", s_ve, s_ve.n)
                p.op("se", "activation", stats[:, 1:2], stats[:, 0:1],
                     AF.Copy, scale=-1.0 / D, inc=(s_ac, 1))
                p.wait("ve", s_ac, s_ac.n)
                p.op("ve", "tensor_scalar", stats[:, 9:10], stats[:, 8:9],
                     0.0, 0.0, ALU.add, ALU.add, accum_out=stats[:, 10:11])
                p.op("ve", "scalar_tensor_tensor", junk[:, :], xst[:, :],
                     stats[:, 1:2], xst[:, :], ALU.add, ALU.mult,
                     accum_out=stats[:, 2:3], inc=(s_ve, 1))
                p.wait("ve", s_ve, s_ve.n)
                p.op("ve", "engine_nop", inc=(s_ve, 1))
                p.wait("se", s_ve, s_ve.n)
                p.op("se", "activation", stats[:, 3:4], stats[:, 2:3], AF.Ln,
                     scale=1.0 / D, bias=stats[:, 24:25])
                p.op("se", "activation", stats[:, 4:5], stats[:, 3:4], AF.Exp,
                     scale=-0.5, inc=(s_ac, 1))
                p.wait("ve", s_ac, s_ac.n)
                p.wait("ve", s_pe, prev_tr_pe)
                if dump is not None:
                    p.wait("ve", s_sx, s_sx.n)
                p.op("ve", "tensor_scalar", xnst[:, :], xst[:, :],
                     stats[:, 1:2], stats[:, 4:5], ALU.add, ALU.mult,
                     inc=(s_ve, 1))
                if dump is not None:
                    p.wait("se", s_ve, s_ve.n)
                    p.op("se", "dma_start",
                         out=dump[src_off + 128 * t:src_off + 128 * (t + 1), :],
                         in_=xnst[:, :], inc=(s_sx, 16))
                p.wait("te", s_ve, s_ve.n)
                for fill in range(4):
                    p.wait("te", s_ac, s_ac.n)
                    for i in range(8):
                        k = fill * 8 + i
                        p.op("te", "transpose", pX[:, i * 128:(i + 1) * 128],
                             xnst[:, k * 128:(k + 1) * 128], ident[:, :],
                             inc=(s_pe, 1) if i == 7 else None)
                    p.wait("se", s_pe, s_pe.n)
                    p.op("se", "activation",
                         ar1_3d[:, fill * 8:(fill + 1) * 8,
                                t * 128:(t + 1) * 128],
                         pX[:, :], AF.Copy, inc=(s_ac, 1))
                prev_tr_pe = s_pe.n

        def xn_chunk(k, c0, w):
            return ar1[:, k * 1024 + c0:k * 1024 + c0 + w]

        def ffn_half(src, dst, res_base, lw_off, bw_ready_sx, dbg_gate=None):
            for t in range(8):
                p.wait("te", s_ve, s_ve.n)
                for k in range(KCH):
                    p.op("te", "matmul", pC[:, 0:8],
                         xn_chunk(k, t * 128, 128), gwT[:, k * 8:(k + 1) * 8],
                         start=(k == 0), stop=(k == KCH - 1),
                         inc=(s_pe, 1) if k == KCH - 1 else None)
                p.wait("ve", s_pe, s_pe.n)
                p.op("ve", "reduce_max", stats[:, 5:6], pC[:, 0:8],
                     axis=AX.X, negate=True, inc=(s_ve, 1))
                p.wait("ve", s_ve, s_ve.n)
                p.op("ve", "tensor_scalar", stats[:, 8:16], pC[:, 0:8],
                     stats[:, 5:6], 0.0, ALU.add, ALU.is_equal, inc=(s_ve, 1))
                p.wait("ve", s_ve, s_ve.n)
                p.op("ve", "scalar_tensor_tensor", stats[:, 16:24],
                     stats[:, 8:16], NEG, pC[:, 0:8], ALU.mult, ALU.add,
                     inc=(s_ve, 1))
                p.wait("ve", s_ve, s_ve.n)
                p.op("ve", "reduce_max", stats[:, 6:7], stats[:, 16:24],
                     axis=AX.X, negate=True, inc=(s_ve, 1))
                p.wait("ve", s_ve, s_ve.n)
                p.op("ve", "tensor_scalar", stats[:, 8:16], pC[:, 0:8],
                     stats[:, 6:7], 0.0, ALU.add, ALU.is_ge, inc=(s_ve, 1))
                p.wait("ve", s_ve, s_ve.n)
                p.op("ve", "tensor_tensor", stats[:, 0:1], stats[:, 5:6],
                     stats[:, 6:7], ALU.subtract, inc=(s_ve, 1))
                p.wait("ve", s_ve, s_ve.n)
                p.op("ve", "engine_nop", inc=(s_ve, 1))
                p.wait("se", s_ve, s_ve.n)
                p.op("se", "activation", stats[:, 16:24], pC[:, 0:8], AF.Exp,
                     bias=stats[:, 5:6])
                p.op("se", "activation", stats[:, 1:2], stats[:, 0:1],
                     AF.Exp, inc=(s_ac, 1))
                p.wait("ve", s_ac, s_ac.n)
                p.op("ve", "tensor_scalar", stats[:, 2:3], stats[:, 1:2],
                     1.0, None, ALU.add, inc=(s_ve, 1))
                p.wait("ve", s_ve, s_ve.n)
                p.op("ve", "reciprocal", stats[:, 3:4], stats[:, 2:3],
                     inc=(s_ve, 1))
                p.wait("ve", s_ve, s_ve.n)
                p.op("ve", "scalar_tensor_tensor", wfull[:, t * 8:(t + 1) * 8],
                     stats[:, 16:24], stats[:, 3:4], stats[:, 8:16],
                     ALU.mult, ALU.mult, inc=(s_ve, 1))
                if dbg_gate is not None and t == 7:
                    p.op("ve", "tensor_copy", est[:, 0:8], pC[:, 0:8],
                         inc=(s_ve, 1))
                    p.wait("ge", s_ve, s_ve.n)
                    p.op("ge", "dma_start", out=dbg_gate[0], in_=est[:, 0:8],
                         inc=(s_lx, 16))
                    p.op("ge", "dma_start", out=dbg_gate[1], in_=stats[:, :],
                         inc=(s_lx, 16))
                    p.wait("te", s_lx, s_lx.n)
            for t in range(8):
                pi = t % 2
                ps = pA if pi == 0 else pB
                p.wait("te", s_ve, psum_free[pi])
                for k in range(KCH):
                    p.op("te", "matmul", ps[:, 0:128],
                         xn_chunk(k, t * 128, 128),
                         aflat[:, k * 128:(k + 1) * 128],
                         start=(k == 0), stop=(k == KCH - 1),
                         inc=(s_pe, 1) if k == KCH - 1 else None)
                wexp = wfull[:, t * 8:(t + 1) * 8].unsqueeze(2) \
                    .broadcast_to([128, 8, 16])
                p.wait("ve", s_pe, s_pe.n)
                p.op("ve", "scalar_tensor_tensor",
                     xnst[:, t * 128:(t + 1) * 128]
                     .rearrange("p (e r) -> p e r", e=8),
                     ps[:, 0:128].rearrange("p (e r) -> p e r", e=8),
                     2.0, wexp, ALU.mult, ALU.mult, inc=(s_ve, 1))
                psum_free[pi] = s_ve.n
                p.wait("te", s_ve, s_ve.n)
                if t == 0:
                    p.wait("te", s_ac, s_ac.n)
                p.op("te", "transpose", pX[:, t * 128:(t + 1) * 128],
                     xnst[:, t * 128:(t + 1) * 128], ident[:, :],
                     inc=(s_pe, 1) if t == 7 else None)
            p.wait("se", s_pe, s_pe.n)
            p.op("se", "activation", lwT[:, lw_off:lw_off + 1024],
                 pX[:, :], AF.Copy, inc=(s_ac, 1))
            lwT_ready = s_ac.n

            bwT_3d = bwT_d.rearrange("(k p) o -> p k o", p=128)
            wbuf_free = [s_pe.n, s_pe.n]
            for Bk in range(8):
                half = Bk % 2
                woff = half * 16384
                p.op("sy", "dma_start",
                     out=ar3[:, woff:woff + 16384]
                     .rearrange("p (k c) -> p k c", k=KCH),
                     in_=bwT_3d[:, :, 512 * Bk:512 * (Bk + 1)],
                     waits=[(s_pe, wbuf_free[half]), (s_sx, bw_ready_sx)],
                     inc=(s_lw, 16))
                lw_need = s_lw.n
                for t in range(8):
                    idx = Bk * 8 + t
                    pi = idx % 2
                    ps = pA if pi == 0 else pB
                    slot = idx % 8
                    p.wait("ge", s_ve, xslot_free[slot])
                    p.op("ge", "dma_start",
                         out=xst[:, (slot % 8) * 512:(slot % 8 + 1) * 512],
                         in_=src[res_base + 128 * t:res_base + 128 * (t + 1),
                                 512 * Bk:512 * (Bk + 1)],
                         inc=(s_lx, 16))
                    need_lx = s_lx.n
                    p.wait("te", s_lw, lw_need)
                    p.wait("te", s_ve, psum_free[pi])
                    if idx == 0:
                        p.wait("te", s_ac, lwT_ready)
                    for k in range(KCH):
                        p.op("te", "matmul", ps[:, :],
                             xn_chunk(k, t * 128, 128),
                             ar3[:, woff + k * 512:woff + (k + 1) * 512],
                             start=(k == 0), stop=False)
                    p.op("te", "matmul", ps[:, :], ones1[:, :],
                         brow[0:1, 512 * Bk:512 * (Bk + 1)],
                         start=False, stop=False)
                    p.op("te", "matmul", ps[:, :],
                         lwT[:, lw_off + t * 128:lw_off + (t + 1) * 128],
                         bflat[:, 512 * Bk:512 * (Bk + 1)],
                         start=False, stop=True, inc=(s_pe, 1))
                    if t == 7:
                        wbuf_free[half] = s_pe.n
                    eslot = idx % 2
                    p.wait("ve", s_pe, s_pe.n)
                    p.wait("ve", s_lx, need_lx)
                    p.wait("ve", s_sx, est_free[eslot])
                    p.op("ve", "scalar_tensor_tensor",
                         est[:, eslot * 512:(eslot + 1) * 512],
                         ps[:, :], 0.5, xst[:, slot * 512:(slot + 1) * 512],
                         ALU.mult, ALU.add, inc=(s_ve, 1))
                    psum_free[pi] = s_ve.n
                    xslot_free[slot] = s_ve.n
                    p.wait("se", s_ve, s_ve.n)
                    p.op("se", "dma_start",
                         out=dst[res_base + 128 * t:res_base + 128 * (t + 1),
                                 512 * Bk:512 * (Bk + 1)],
                         in_=est[:, eslot * 512:(eslot + 1) * 512],
                         inc=(s_sx, 16))
                    est_free[eslot] = s_sx.n

        wst_cast_done = [0, 0]

        def stream_to_ar3(w3d, kslice, cols, dst_off, region_free_pe):
            k0, k1 = kslice
            per = max(1, 1024 // cols)
            kk = k0
            pc = 0
            while kk < k1:
                kn = min(per, k1 - kk)
                buf = wst if pc % 2 == 0 else wst2
                eng = "ge" if pc % 2 == 0 else "ve"
                sem = s_gp if pc % 2 == 0 else s_ve
                p.op("sy", "dma_start",
                     out=buf[:, 0:kn * cols].rearrange("p (k c) -> p k c", k=kn),
                     in_=w3d[:, kk:kk + kn, :],
                     waits=[(s_gp if pc % 2 == 0 else s_ve,
                             wst_cast_done[pc % 2]),
                            (s_pe, region_free_pe)],
                     inc=(s_lw, 16))
                p.wait(eng, s_lw, s_lw.n)
                p.wait(eng, s_pe, region_free_pe)
                p.op(eng, "tensor_copy",
                     ar3[:, dst_off + (kk - k0) * cols:
                         dst_off + (kk - k0 + kn) * cols],
                     buf[:, 0:kn * cols], inc=(sem, 1))
                wst_cast_done[pc % 2] = sem.n
                kk += kn
                pc += 1
            return (s_gp.n, s_ve.n)

        def proj_pass(w3d, colslice, hf, out_dram, out_rowmajor, scale):
            pass

        prologue_sx = s_sx.n
        if stage >= 1:
            ln_tiles(x_in, 0, 8)
            fence()
        if stage >= 2:
            ffn_half(x_in, x1_d, 0, 0, prologue_sx,
                     dbg_gate=(dbg['dbg_gl'], dbg['dbg_gs']) if debug else None)
            fence()
            if debug:
                p.op("ge", "dma_start", out=dbg["dbg_wf"], in_=wfull[:, :],
                     inc=(s_lx, 16))
                p.op("ge", "dma_start", out=dbg["dbg_lwT"], in_=lwT[:, :],
                     inc=(s_lx, 16))
                fence()
        if stage >= 3:
            ln_tiles(x_in, 1024, 8)
            fence()
            ffn_half(x_in, x1_d, 1024, 1024, prologue_sx)
            fence()

        wkv3d = wkv_d.rearrange("(k p) c -> p k c", p=128)
        wq3d = wq_d.rearrange("(k p) c -> p k c", p=128)
        wo3d = wo_d.rearrange("(k p) c -> p k c", p=128)

        for hf in range(2 if stage >= 4 else 0):
            ln_tiles(x1_d, hf * 1024, 8, dump=dbg.get("dbg_xn"))
            fence()
            reg_pe = [0, 0]
            ac_free = {0: 0, 1: 0}
            gp_need_arr = [0] * (H + 1)
            gp_need_arr[0] = stream_to_ar3(wkv3d[:, :, 0:128], (0, KCH),
                                           128, 0, 0)
            for h in range(H):
                if h + 1 < H:
                    gp_need_arr[h + 1] = stream_to_ar3(
                        wkv3d[:, :, 128 * (h + 1):128 * (h + 2)],
                        (0, KCH), 128, ((h + 1) % 2) * 4096,
                        reg_pe[(h + 1) % 2])
                roff = (h % 2) * 4096
                for tq in range(2):
                    pi = tq % 2
                    ps = pA if pi == 0 else pB
                    p.wait("te", s_gp, gp_need_arr[h][0])
                    p.wait("te", s_ve, gp_need_arr[h][1])
                    p.wait("te", s_ac, ac_free[pi])
                    for k in range(KCH):
                        p.op("te", "matmul", ps[:, :],
                             ar3[:, roff + k * 128:roff + (k + 1) * 128],
                             xn_chunk(k, tq * 512, 512),
                             start=(k == 0), stop=(k == KCH - 1),
                             inc=(s_pe, 1) if k == KCH - 1 else None)
                    reg_pe[h % 2] = s_pe.n
                    bslot = (h * 2 + tq) % 4
                    p.wait("se", s_pe, s_pe.n)
                    p.wait("se", s_sx, bst_free[bslot])
                    p.op("se", "activation",
                         bst[:, bslot * 512:(bslot + 1) * 512], ps[:, :],
                         AF.Copy, inc=(s_ac, 1))
                    ac_free[pi] = s_ac.n
                    p.op("se", "dma_start",
                         out=kT_d[128 * h:128 * (h + 1),
                                  hf * 1024 + tq * 512:
                                  hf * 1024 + (tq + 1) * 512],
                         in_=bst[:, bslot * 512:(bslot + 1) * 512],
                         inc=(s_sx, 16))
                    bst_free[bslot] = s_sx.n
            fence()
            reg_pe = [0, 0]
            ac_free = {0: 0, 1: 0}
            gp_need_arr = [0] * 9
            gp_need_arr[0] = stream_to_ar3(wkv3d[:, :, 4096:4608], (0, KCH),
                                           512, 0, 0)
            for vb in range(8):
                if vb + 1 < 8:
                    gp_need_arr[vb + 1] = stream_to_ar3(
                        wkv3d[:, :, 4096 + 512 * (vb + 1):4096 + 512 * (vb + 2)],
                        (0, KCH), 512, ((vb + 1) % 2) * 16384,
                        reg_pe[(vb + 1) % 2])
                roff = (vb % 2) * 16384
                for t in range(8):
                    pi = t % 2
                    ps = pA if pi == 0 else pB
                    p.wait("te", s_gp, gp_need_arr[vb][0])
                    p.wait("te", s_ve, gp_need_arr[vb][1])
                    p.wait("te", s_ac, ac_free[pi])
                    for k in range(KCH):
                        p.op("te", "matmul", ps[:, :],
                             xn_chunk(k, t * 128, 128),
                             ar3[:, roff + k * 512:roff + (k + 1) * 512],
                             start=(k == 0), stop=(k == KCH - 1),
                             inc=(s_pe, 1) if k == KCH - 1 else None)
                    reg_pe[vb % 2] = s_pe.n
                    bslot = t % 4
                    p.wait("se", s_pe, s_pe.n)
                    p.wait("se", s_sx, bst_free[bslot])
                    p.op("se", "activation",
                         bst[:, bslot * 512:(bslot + 1) * 512], ps[:, :],
                         AF.Copy, inc=(s_ac, 1))
                    ac_free[pi] = s_ac.n
                    p.op("se", "dma_start",
                         out=v_d[hf * 1024 + 128 * t:hf * 1024 + 128 * (t + 1),
                                 512 * vb:512 * (vb + 1)],
                         in_=bst[:, bslot * 512:(bslot + 1) * 512],
                         inc=(s_sx, 16))
                    bst_free[bslot] = s_sx.n
            fence()
            if hf == 0:
                reg_pe = [0, 0]
                ac_free = {0: 0, 1: 0}
                gp_need_arr = [0] * (H + 1)
                gp_need_arr[0] = stream_to_ar3(wq3d[:, :, 0:128], (0, KCH),
                                               128, 0, 0)
                for h in range(H):
                    if h + 1 < H:
                        gp_need_arr[h + 1] = stream_to_ar3(
                            wq3d[:, :, 192 * (h + 1):192 * (h + 1) + 128],
                            (0, KCH), 128, ((h + 1) % 2) * 4096,
                            reg_pe[(h + 1) % 2])
                    roff = (h % 2) * 4096
                    for tq in range(2):
                        pi = tq % 2
                        ps = pA if pi == 0 else pB
                        p.wait("te", s_gp, gp_need_arr[h][0])
                        p.wait("te", s_ve, gp_need_arr[h][1])
                        p.wait("te", s_ac, ac_free[pi])
                        for k in range(KCH):
                            p.op("te", "matmul", ps[:, :],
                                 ar3[:, roff + k * 128:roff + (k + 1) * 128],
                                 xn_chunk(k, tq * 512, 512),
                                 start=(k == 0), stop=(k == KCH - 1),
                                 inc=(s_pe, 1) if k == KCH - 1 else None)
                        reg_pe[h % 2] = s_pe.n
                        bslot = (h * 2 + tq) % 4
                        p.wait("se", s_pe, s_pe.n)
                        p.wait("se", s_sx, bst_free[bslot])
                        p.op("se", "activation",
                             bst[:, bslot * 512:(bslot + 1) * 512], ps[:, :],
                             AF.Copy, scale=SCALE_Q, inc=(s_ac, 1))
                        ac_free[pi] = s_ac.n
                        p.op("se", "dma_start",
                             out=qT_d[128 * h:128 * (h + 1),
                                      tq * 512:(tq + 1) * 512],
                             in_=bst[:, bslot * 512:(bslot + 1) * 512],
                             inc=(s_sx, 16))
                        bst_free[bslot] = s_sx.n
                fence()
        fence()

        H_RUN = (1 if stage == 45 else (H if stage >= 5 else 0))
        PSB, PT = 10240, 12288
        pt3 = ar3[:, PT:PT + 16384].rearrange("p (c q) -> p c q", c=16)
        v3d = v_d.rearrange("(c p) v -> p c v", p=128)
        p.op("ve", "memset", stats[:, 5:6], 0.0)
        scores_done = s_pe.n
        pv_done = s_pe.n
        pt_dump_lx = [0]
        psb_dump_lx = [0]
        sc_dump_lx = [0]
        pt_evac_done = s_ac.n
        attn_evac = {0: s_ac.n, 1: s_ac.n}
        pv_done_arr = [0] * (H + 1)
        loads_arr = [0] * (H + 1)

        def attn_loads(hh, wait_pe):
            i = hh % 2
            p.wait("ge", s_pe, wait_pe)
            p.op("ge", "dma_start", out=ar3[:, i * 2048:i * 2048 + 2048],
                 in_=kT_d[128 * hh:128 * (hh + 1), :], inc=(s_lx, 16))
            p.op("ge", "dma_start",
                 out=ar3[:, 4096 + i * 2048:4096 + i * 2048 + 2048]
                 .rearrange("p (c v) -> p c v", c=16),
                 in_=v3d[:, :, 128 * hh:128 * (hh + 1)], inc=(s_lx, 16))
            p.op("ge", "dma_start",
                 out=ar3[:, 8192 + i * 1024:8192 + i * 1024 + 1024],
                 in_=qT_d[128 * hh:128 * (hh + 1), :], inc=(s_lx, 16))
            loads_arr[hh] = s_lx.n

        if H_RUN > 0:
            attn_loads(0, 0)
        for h in range(H_RUN):
            KH = (h % 2) * 2048
            VH = 4096 + (h % 2) * 2048
            QH = 8192 + (h % 2) * 1024
            if h + 1 < H_RUN:
                attn_loads(h + 1, pv_done_arr[h - 1] if h >= 1 else 0)
            loads_done = loads_arr[h]
            exp_done_prev = s_ac.n

            def emit_scores(l):
                gg = l // 2
                kw = 256 * (gg + 1)
                p.wait("te", s_lx, loads_done)
                p.wait("te", s_ac, exp_done_prev)
                pieces = []
                for (a, b) in bank_pieces(0, kw):
                    pieces.append((a, b, ar3[:, KH + a:KH + b]))
                for (a, b) in bank_pieces(kw, 2 * kw):
                    pieces.append((a, b,
                                   ar3[:, KH + 1024 + (a - kw):
                                       KH + 1024 + (b - kw)]))
                banks = set()
                for (a, b, rhs) in pieces:
                    p.op("te", "matmul", pS[:, a:b],
                         ar3[:, QH + 128 * l:QH + 128 * (l + 1)], rhs,
                         start=(a // 512 not in banks), stop=True)
                    banks.add(a // 512)
                for w, (ma, mb) in enumerate(((kw - 256, kw),
                                              (2 * kw - 256, 2 * kw))):
                    p.op("te", "matmul", pS[:, ma:mb], ident[:, :],
                         masksb[:, (2 * l + w) * 256:(2 * l + w + 1) * 256],
                         start=False, stop=True, skip_group_check=True,
                         inc=(s_pe, 1) if w == 1 else None)

            emit_scores(0)
            for l in range(8):
                gg = l // 2
                kw = 256 * (gg + 1)
                p.wait("ve", s_pe, s_pe.n)
                p.op("ve", "reduce_max", stats[:, 5:6], pS[:, 0:2 * kw],
                     axis=AX.X, negate=True, inc=(s_ve, 1))
                p.wait("ve", s_ve, s_ve.n)
                p.op("ve", "engine_nop", inc=(s_ve, 1))
                p.wait("se", s_ve, s_ve.n)
                p.op("se", "activation", ar3[:, PSB:PSB + 2 * kw],
                     pS[:, 0:2 * kw], AF.Exp, bias=stats[:, 5:6],
                     inc=(s_ac, 1))
                exp_done_prev = s_ac.n
                p.wait("ve", s_ac, s_ac.n)
                p.op("ve", "reduce_sum", stats[:, 6:7],
                     ar3[:, PSB:PSB + 2 * kw], axis=AX.X, inc=(s_ve, 1))
                p.wait("ve", s_ve, s_ve.n)
                p.op("ve", "reciprocal", stats[:, 7:8], stats[:, 6:7],
                     inc=(s_ve, 1))
                p.wait("ve", s_ve, s_ve.n)
                p.op("ve", "tensor_scalar", ar3[:, PSB:PSB + 2 * kw],
                     ar3[:, PSB:PSB + 2 * kw], stats[:, 7:8], None, ALU.mult,
                     inc=(s_ve, 1))
                scale_done = s_ve.n
                if l < 7:
                    emit_scores(l + 1)
                p.wait("te", s_ve, scale_done)
                for w in range(2):
                    nch = 2 * gg + 2
                    p.wait("te", s_ac, s_ac.n)
                    for ci in range(nch):
                        src_c = ci if w == 0 else (2 * gg + 2 + ci)
                        p.op("te", "transpose",
                             pX[:, ci * 128:(ci + 1) * 128],
                             ar3[:, PSB + src_c * 128:PSB + (src_c + 1) * 128],
                             ident[:, :],
                             inc=(s_pe, 1) if ci == nch - 1 else None)
                    p.wait("se", s_pe, s_pe.n)
                    c0 = 0 if w == 0 else 8
                    p.op("se", "activation",
                         pt3[:, c0:c0 + nch, 128 * l:128 * (l + 1)],
                         pX[:, 0:nch * 128], AF.Copy, inc=(s_ac, 1))
            pt_evac_done = s_ac.n
            if debug and h == 0:
                p.wait("ge", s_ac, pt_evac_done)
                p.op("ge", "dma_start", out=dbg["dbg_pt"],
                     in_=ar3[:, PT:PT + 16384], inc=(s_lx, 16))
                pt_dump_lx[0] = s_lx.n
            if debug and h == 1:
                p.wait("se", s_lx, pt_dump_lx[0])
            p.wait("te", s_ac, pt_evac_done)
            p.wait("te", s_ac, attn_evac[0])
            p.wait("te", s_ac, attn_evac[1])
            for c in range(16):
                lmin = 2 * ((c % 8) // 2)
                lhsT = ar3[:, VH + c * 128:VH + (c + 1) * 128]
                if lmin < 4:
                    p.op("te", "matmul", pA[:, lmin * 128:512], lhsT,
                         ar3[:, PT + c * 1024 + lmin * 128:PT + c * 1024 + 512],
                         start=(c == 0), stop=(c == 11),
                         inc=(s_pe, 1) if c == 11 else None)
                lm4 = max(lmin, 4)
                p.op("te", "matmul", pB[:, (lm4 - 4) * 128:512], lhsT,
                     ar3[:, PT + c * 1024 + lm4 * 128:PT + c * 1024 + 1024],
                     start=(c == 0), stop=(c == 15),
                     inc=(s_pe, 1) if c == 15 else None)
            pv_done = s_pe.n
            pv_done_arr[h] = pv_done
            p.wait("se", s_pe, pv_done)
            p.op("se", "activation", ar1[:, h * 1024:h * 1024 + 512],
                 pA[:, :], AF.Copy, inc=(s_ac, 1))
            attn_evac[0] = s_ac.n
            p.op("se", "activation", ar1[:, h * 1024 + 512:(h + 1) * 1024],
                 pB[:, :], AF.Copy, inc=(s_ac, 1))
            attn_evac[1] = s_ac.n
        fence()
        if debug:
            p.op("ge", "dma_start", out=dbg["dbg_at"], in_=ar1[:, :],
                 inc=(s_lx, 16))
            p.op("ge", "dma_start", out=dbg["dbg_pt"],
                 in_=ar3[:, PT:PT + 16384], inc=(s_lx, 16))
            p.op("ge", "dma_start", out=dbg["dbg_psb"],
                 in_=ar3[:, PSB:PSB + 2048], inc=(s_lx, 16))
            p.op("ge", "dma_start", out=dbg["dbg_ss"],
                 in_=stats[:, :], inc=(s_lx, 16))
            fence()

        NBK_WO = 8 if stage >= 6 else 0
        psum_free = {0: 0, 1: 0}
        est_free = [s_sx.n] * 2
        xslot_free = [s_ve.n] * 8
        wbuf_free = [s_pe.n, s_pe.n]
        for Bk in range(NBK_WO):
            half = Bk % 2
            woff = half * 16384
            gp_need = stream_to_ar3(wo3d[:, :, 512 * Bk:512 * (Bk + 1)],
                                    (0, KCH), 512, woff, wbuf_free[half])
            for t in range(8):
                idx = Bk * 8 + t
                pi = idx % 2
                ps = pA if pi == 0 else pB
                slot = idx % 8
                p.wait("ge", s_ve, xslot_free[slot])
                p.op("ge", "dma_start",
                     out=xst[:, slot * 512:(slot + 1) * 512],
                     in_=x1_d[128 * t:128 * (t + 1), 512 * Bk:512 * (Bk + 1)],
                     inc=(s_lx, 16))
                need_lx = s_lx.n
                p.wait("te", s_gp, gp_need[0])
                p.wait("te", s_ve, gp_need[1])
                p.wait("te", s_ve, psum_free[pi])
                for k in range(KCH):
                    p.op("te", "matmul", ps[:, :],
                         ar1[:, k * 1024 + t * 128:k * 1024 + (t + 1) * 128],
                         ar3[:, woff + k * 512:woff + (k + 1) * 512],
                         start=(k == 0), stop=(k == KCH - 1),
                         inc=(s_pe, 1) if k == KCH - 1 else None)
                if t == 7:
                    wbuf_free[half] = s_pe.n
                eslot = idx % 2
                p.wait("ve", s_pe, s_pe.n)
                p.wait("ve", s_lx, need_lx)
                p.wait("ve", s_sx, est_free[eslot])
                p.op("ve", "scalar_tensor_tensor",
                     est[:, eslot * 512:(eslot + 1) * 512],
                     ps[:, :], 1.0, xst[:, slot * 512:(slot + 1) * 512],
                     ALU.mult, ALU.add, inc=(s_ve, 1))
                psum_free[pi] = s_ve.n
                xslot_free[slot] = s_ve.n
                p.wait("se", s_ve, s_ve.n)
                p.op("se", "dma_start",
                     out=x2_d[128 * t:128 * (t + 1), 512 * Bk:512 * (Bk + 1)],
                     in_=est[:, eslot * 512:(eslot + 1) * 512],
                     inc=(s_sx, 16))
                est_free[eslot] = s_sx.n
        fence()

        psum_free = {0: 0, 1: 0}
        if stage >= 7:
            ln_tiles(x2_d, 0, 8)
            fence()
            ffn_half(x2_d, y_d, 0, 0, prologue_sx)
            fence()

        if debug:
            p.op("ge", "dma_start", out=dbg["dbg_x1"], in_=x1_d, inc=(s_lx, 16))
            p.op("ge", "dma_start", out=dbg["dbg_x2"], in_=x2_d, inc=(s_lx, 16))
            p.op("ge", "dma_start", out=dbg["dbg_kT"], in_=kT_d, inc=(s_lx, 16))
            p.op("ge", "dma_start", out=dbg["dbg_v"], in_=v_d, inc=(s_lx, 16))
            p.op("ge", "dma_start", out=dbg["dbg_qT"], in_=qT_d, inc=(s_lx, 16))
            p.op("ge", "dma_start", out=dbg["dbg_bwT"], in_=bwT_d, inc=(s_lx, 16))
            fence()

        with nc.Block() as block:
            @block.tensor
            def _(te):
                p.run(te, "te")

            @block.vector
            def _(ve):
                p.run(ve, "ve")

            @block.scalar
            def _(se):
                p.run(se, "se")

            @block.gpsimd
            def _(ge):
                p.run(ge, "ge")

            @block.sync
            def _(sy):
                p.run(sy, "sy")

    return nc


def make_in_maps(x, wq, wkv, wo, gate, base, bias, lA, lB):
    ident = np.eye(128).astype(ml_dtypes.bfloat16)
    ones1 = np.ones((1, 128)).astype(ml_dtypes.bfloat16)
    masks = [np.ascontiguousarray(build_masks(0).reshape(16 * 128, 256)),
             np.ascontiguousarray(build_masks(1).reshape(16 * 128, 256))]
    in_maps = []
    for c in range(8):
        s, par = c // 2, c % 2
        ts = own_tiles(par) + own_tiles(1 - par)
        rows = np.concatenate([np.arange(128 * t, 128 * (t + 1)) for t in ts])
        in_maps.append({
            "x_in": np.ascontiguousarray(x[s][rows]),
            "wq": wq, "wkv": wkv, "wo": wo,
            "gate_w": gate, "base_w": base, "base_b": bias,
            "lora_A": lA.reshape(E * R, D), "lora_B": lB,
            "ident": ident, "ones1": ones1,
            "masks": masks[par],
        })
    return in_maps


def kernel(L0_attn_hidden_states, L0_attn_wq, L0_attn_wkv, L0_attn_wo,
           L0_ffn_gate_w, L0_ffn_base_w, L0_ffn_base_b, L0_ffn_lora_A,
           L0_ffn_lora_B):
    cvt = lambda a: np.ascontiguousarray(np.asarray(a, np.float32))
    x = cvt(L0_attn_hidden_states)
    in_maps = make_in_maps(
        x, cvt(L0_attn_wq), cvt(L0_attn_wkv), cvt(L0_attn_wo),
        cvt(L0_ffn_gate_w), cvt(L0_ffn_base_w),
        cvt(L0_ffn_base_b).reshape(1, D), cvt(L0_ffn_lora_A),
        cvt(L0_ffn_lora_B))
    nc = build_program(debug=False)
    res = run_bass_kernel_spmd(nc, in_maps, core_ids=list(range(8)))
    out = np.empty((4, S, D), np.float32)
    for c in range(8):
        s, par = c // 2, c % 2
        rows = np.concatenate(
            [np.arange(128 * t, 128 * (t + 1)) for t in own_tiles(par)])
        out[s][rows] = res.results[c]["y"]
    return out
